# version 18
# speedup vs baseline: 1.5853x; 1.0668x over previous
import numpy as np
import ml_dtypes
import concourse.bass as bass
import concourse.mybir as mybir
from concourse.bass_utils import run_bass_kernel_spmd

F32 = mybir.dt.float32
BF16 = mybir.dt.bfloat16
AF = mybir.ActivationFunctionType
ALU = mybir.AluOpType
AX = mybir.AxisListType

D = 1024
NC = 8
NH = 16
HS = 64
KVL = 512
G = 128
WIN = 256
DI = 2730
NKI = 22
DIP = NKI * 128
NEG = -1e30

bfd = ml_dtypes.bfloat16

ENGS = ("pe", "dve", "act", "gp", "sq", "gq")



def host_tables(T):
    t = {}
    t["ident"] = np.eye(128, dtype=bfd)
    P32 = np.zeros((128, 128), np.float32)
    for e in range(128):
        grp, d0 = (e // 64) * 64, e % 64
        P32[e, grp + (d0 + 32) % 64] = 1.0
    t["p32"] = P32.astype(bfd)
    gs = np.zeros((128, 2), np.float32)
    gs[:64, 0] = 1.0
    gs[64:, 1] = 1.0
    t["grpsel"] = gs.astype(bfd)
    s2 = np.zeros((2, 128), np.float32)
    s2[0, :64] = 1.0
    s2[1, 64:] = 1.0
    t["sel2"] = s2
    cm = np.zeros((4, 128, 512), np.float32)
    for vi, r in enumerate((0, 128, 256, 384)):
        for i in range(128):
            cm[vi, i, r + i + 1:] = NEG
    t["cmask"] = np.ascontiguousarray(cm.transpose(1, 0, 2).reshape(128, 4 * 512)).astype(bfd)
    bm = np.zeros((4, 128, 512), np.float32)
    for w0 in (0, 1):
        for qt in (0, 1):
            for i in range(128):
                qa = 256 + qt * 128 + i
                for j in range(512):
                    ok = (0 <= qa - j <= WIN) and not (w0 == 1 and j < 256)
                    bm[w0 * 2 + qt, i, j] = 0.0 if ok else NEG
    t["bmask"] = np.ascontiguousarray(bm.transpose(1, 0, 2).reshape(128, 4 * 512)).astype(bfd)
    mt = np.zeros((128, 3 * 128), np.float32)
    for kc in range(3):
        for p in range(128):
            j = kc * 128 + p
            for i in range(128):
                if i <= j <= i + 256:
                    mt[p, kc * 128 + i] = 1.0
    t["maskt"] = mt.astype(bfd)
    half = np.arange(0, HS, 2, dtype=np.float64)
    inv_freq = 1.0 / 10000.0 ** (half / HS)
    tt = np.arange(T, dtype=np.float64)
    fr = tt[:, None] * inv_freq[None, :]
    fr = np.concatenate([fr, fr], -1)
    base_scale = (half + 0.4 * HS) / (1.4 * HS)
    power = (tt - T // 2) / (WIN // 2)
    sc = base_scale[None, :] ** power[:, None]
    sc = np.concatenate([sc, sc], -1)
    cs = (np.cos(fr) * sc).T
    ss = (np.sin(fr) * sc).T
    ci = (np.cos(fr) / sc).T
    si = (np.sin(fr) / sc).T
    perm = np.array([(e + 32) % 64 for e in range(64)])
    sgn = np.where(np.arange(64) < 32, 1.0, -1.0)[:, None]
    rt = np.concatenate([cs, sgn * ss[perm], ci, sgn * si[perm]], axis=1)
    t["rot"] = np.ascontiguousarray(np.vstack([rt, rt]).astype(bfd))
    return t


def _bias_cols(b, pad_to=None):
    b = np.asarray(b, np.float32).reshape(-1)
    if pad_to:
        b = np.pad(b, (0, pad_to - len(b)))
    nch = (len(b) + 127) // 128
    out = np.zeros((128, nch), np.float32)
    for c in range(nch):
        seg = b[c * 128:(c + 1) * 128]
        out[: len(seg), c] = seg
    return out


def _img(w):
    K, M = w.shape
    assert K % 128 == 0
    return np.ascontiguousarray(
        w.reshape(K // 128, 128, M).transpose(1, 0, 2).reshape(128, (K // 128) * M)
    ).astype(np.float32)


def prep_weights(iw):
    w = {}
    f32 = lambda a: np.ascontiguousarray(np.asarray(a), np.float32)
    w["l0a_wdq"] = _img(f32(iw["L0_attn_W_dq"]))
    w["l0a_wuqT"] = _img(f32(iw["L0_attn_W_uq"]).T)
    w["l0a_wuk"] = _img(f32(iw["L0_attn_W_uk"]))
    w["l0a_wuv"] = _img(f32(iw["L0_attn_W_uv"]))
    w["l0a_woT"] = _img(f32(iw["L0_attn_W_o"]).T)
    w["l0a_wdkvT"] = _img(f32(iw["L0_attn_W_dkv"]).T)
    w["l0f_cn_w1"] = _img(f32(iw["L0_ffn_cn_w1"]))
    w["l0f_cn_w2"] = _img(f32(iw["L0_ffn_cn_w2"]))
    w["l0f_cn_w3"] = _img(f32(iw["L0_ffn_cn_w3"]))
    def pad128(a):
        o = np.zeros((a.shape[0], 128), np.float32)
        o[:, :a.shape[1]] = a
        return o
    w["l0f_cn_w4"] = _img(pad128(f32(iw["L0_ffn_cn_w4"])))
    w["l0f_gc_w1"] = _img(f32(iw["L0_ffn_gc_w1"]))
    w["l0f_gc_w2"] = _img(f32(iw["L0_ffn_gc_w2"]))
    w["l0f_gc_w3"] = _img(pad128(f32(iw["L0_ffn_gc_w3"])))
    w["l0f_cn_b1"] = _bias_cols(iw["L0_ffn_cn_b1"])
    w["l0f_cn_b2"] = _bias_cols(iw["L0_ffn_cn_b2"])
    w["l0f_cn_b3"] = _bias_cols(iw["L0_ffn_cn_b3"])
    w["l0f_cn_b4"] = f32(iw["L0_ffn_cn_b4"]).reshape(1, 1)
    w["l0f_gc_b1"] = _bias_cols(iw["L0_ffn_gc_b1"])
    w["l0f_gc_b2"] = _bias_cols(iw["L0_ffn_gc_b2"])
    w["l0f_gc_b3"] = f32(iw["L0_ffn_gc_b3"]).reshape(1, 1)
    w["l0f_ar"] = f32(iw["L0_ffn_adaptive_range"]).reshape(1, 1)
    w["l1a_wqkv"] = _img(f32(iw["L1_attn_w_qkv"]))
    w["l1a_wproj"] = _img(f32(iw["L1_attn_w_proj"]))
    w["l1a_bproj"] = _bias_cols(iw["L1_attn_b_proj"])
    w["l1a_g"] = _bias_cols(iw["L1_attn_ln_gamma"])
    w["l1a_b"] = _bias_cols(iw["L1_attn_ln_beta"])
    w["l1f_wqkv"] = np.stack([_img(f32(iw["L1_ffn_Wqkv"][i])) for i in range(6)])
    w["l1f_wo"] = np.stack([_img(f32(iw["L1_ffn_Wo"][i])) for i in range(6)])
    qk = np.zeros((128, 6 * 4), np.float32)
    for i in range(6):
        qk[:64, i * 4 + 0] = f32(iw["L1_ffn_q_scale"][i])
        qk[64:, i * 4 + 0] = f32(iw["L1_ffn_q_scale"][i])
        qk[:64, i * 4 + 1] = f32(iw["L1_ffn_k_scale"][i])
        qk[64:, i * 4 + 1] = f32(iw["L1_ffn_k_scale"][i])
    w["l1f_qsks"] = qk
    w["l1f_g"] = np.stack([_bias_cols(iw["L1_ffn_ln_g"][i]) for i in range(6)])
    w["l1f_b"] = np.stack([_bias_cols(iw["L1_ffn_ln_b"][i]) for i in range(6)])
    W1 = f32(iw["L1_ffn_W1"])
    w1a = np.zeros((6, D, DIP), np.float32)
    w1g = np.zeros((6, D, DIP), np.float32)
    w1a[:, :, :DI] = W1[:, :, :DI]
    w1g[:, :, :DI] = W1[:, :, DI:]
    def img_hc(W):
        return np.ascontiguousarray(
            W.reshape(8, 128, NKI, 128).transpose(1, 2, 0, 3).reshape(128, NKI * 1024)
        ).astype(np.float32)
    w["l1f_w1a"] = np.stack([img_hc(w1a[i]) for i in range(6)])
    w["l1f_w1g"] = np.stack([img_hc(w1g[i]) for i in range(6)])
    W2 = f32(iw["L1_ffn_W2"])
    w2p = np.zeros((6, DIP, D), np.float32)
    w2p[:, :DI, :] = W2
    w["l1f_w2"] = np.stack([_img(w2p[i]) for i in range(6)])
    b1 = f32(iw["L1_ffn_b1"])
    w["l1f_b1a"] = np.stack([_bias_cols(b1[i, :DI], DIP) for i in range(6)])
    w["l1f_b1g"] = np.stack([_bias_cols(b1[i, DI:], DIP) for i in range(6)])
    w["l1f_b2"] = np.stack([_bias_cols(iw["L1_ffn_b2"][i]) for i in range(6)])
    return w



NPOOL = 8

class Prog:
    def __init__(self, nc):
        self.nc = nc
        self.clocks = list(ENGS) + [f"{q}{i}" for q in ("sq", "gq")
                                    for i in range(NPOOL)]
        self.plan = {e: [] for e in ENGS}
        self.seq = {c: 0 for c in self.clocks}
        self.known = {e: {} for e in ENGS}
        self.snap = {c: {} for c in self.clocks}
        self.dep_w = {}
        self.dep_r = {}
        self.rr = {"sq": 0, "gq": 0}
        self.order = []
        self.fence_v = {}

    def _merge_need(self, need, lst):
        for (e, s) in lst:
            if need.get(e, 0) < s:
                need[e] = s

    def step(self, eng, fn, reads=(), writes=(), ndma=0, tag=None):
        need = {}
        self._merge_need(need, self.fence_v.items())
        for r in reads:
            self._merge_need(need, self.dep_w.get(r, ()))
        for r in writes:
            self._merge_need(need, self.dep_w.get(r, ()))
            self._merge_need(need, self.dep_r.get(r, ()))
        if ndma:
            clock = f"{eng}{self.rr[eng] % NPOOL}"
            self.rr[eng] += 1
            if self.seq[clock] > 0:
                need[clock] = max(need.get(clock, 0), self.seq[clock])
        else:
            clock = eng
        waits = []
        kn = self.known[eng]
        for pe_, s_ in sorted(need.items()):
            if kn.get(pe_, 0) >= s_:
                continue
            waits.append((pe_, s_))
            psnap = self.snap[pe_].get(s_)
            if psnap:
                for e2, v2 in psnap.items():
                    if kn.get(e2, 0) < v2:
                        kn[e2] = v2
            if kn.get(pe_, 0) < s_:
                kn[pe_] = s_
        self.seq[clock] += 16 * ndma if ndma else 1
        s_now = self.seq[clock]
        self.snap[clock][s_now] = dict(kn, **{clock: s_now})
        me = [(clock, s_now)]
        for r in reads:
            self.dep_r.setdefault(r, []).append((clock, s_now))
        for r in writes:
            self.dep_w[r] = list(me)
            self.dep_r[r] = []
        self.plan[eng].append((waits, fn, ndma, clock, tag))
        self.order.append((eng, len(self.plan[eng]) - 1))

    def claim(self, old_keys, new_keys):
        acc = {}
        for r in old_keys:
            self._merge_need(acc, self.dep_w.pop(r, ()))
            self._merge_need(acc, self.dep_r.pop(r, ()))
        lst = list(acc.items())
        for r in new_keys:
            self.dep_w[r] = lst
            self.dep_r[r] = []

    def fence(self, keys):
        acc = dict(self.fence_v)
        for r in keys:
            self._merge_need(acc, self.dep_w.pop(r, ()))
            self._merge_need(acc, self.dep_r.pop(r, ()))
        self.fence_v = acc

    def fence_all(self):
        self.fence_v = {c: self.seq[c] for c in self.clocks if self.seq[c] > 0}

    def finish(self, eng):
        waits = [(c, self.seq[c]) for c in self.clocks
                 if c.startswith(eng) and c != eng and self.seq[c] > 0]
        self.plan[eng].append((waits, None, 0, eng, "fin"))
        self.order.append((eng, len(self.plan[eng]) - 1))

    def emit(self):
        nc = self.nc
        from contextlib import ExitStack
        ctx = ExitStack()
        sems = {c: ctx.enter_context(nc.semaphore("sem_" + c))
                for c in self.clocks if self.seq[c] > 0 or c in ENGS}
        blk = ctx.enter_context(nc.Block())

        def body_for(names):
            steps = [(e, i) for (e, i) in self.order if e in names]

            def run(engobj):
                for (e, i) in steps:
                    waits, fn, ndma, clock, tag = self.plan[e][i]
                    for (pe_, v_) in waits:
                        engobj.wait_ge(sems[pe_], v_)
                    if fn is None:
                        continue
                    ret = fn(engobj)
                    if ndma:
                        insts = ret if isinstance(ret, (list, tuple)) else [ret]
                        assert len(insts) == ndma, (tag, len(insts), ndma)
                        for ins in insts:
                            ins.then_inc(sems[clock], 16)
                    else:
                        assert ret is not None, (e, i, tag)
                        ret.then_inc(sems[clock], 1)
            return run

        blk.tensor(body_for(("pe",)))
        blk.vector(body_for(("dve",)))
        blk.scalar(body_for(("act",)))
        blk.gpsimd(body_for(("gp", "gq")))
        blk.sync(body_for(("sq",)))
        ctx.close()



class MK:

    def __init__(self, T=2048, blocks=("l0a", "l0f", "l1a", "l1f"), nseq=2):
        self.T = T
        self.TG = T + G
        self.NTB = T // 512
        self.blocks = blocks
        self.nseq = nseq
        self.nc = bass.Bass("TRN2", target_bir_lowering=False)
        self.p = Prog(self.nc)
        self.dram = {}
        self.in_names = []
        self._guards = {}
        self.build()

    def din(self, name, shape, dtype=F32):
        if name not in self.dram:
            self.dram[name] = self.nc.dram_tensor(name, list(shape), dtype,
                                                  kind="ExternalInput").ap()
            self.in_names.append(name)
        return self.dram[name]

    def dout(self, name, shape, dtype=F32):
        self.dram[name] = self.nc.dram_tensor(name, list(shape), dtype,
                                              kind="ExternalOutput").ap()
        return self.dram[name]

    def dtmp(self, name, shape, dtype=F32):
        if name not in self.dram:
            self.dram[name] = self.nc.dram_tensor(name, list(shape), dtype,
                                                  kind="Internal").ap()
        return self.dram[name]

    def alloc(self, name, shape, dtype, side="left"):
        g = self.nc.sbuf_tensor(name, list(shape), dtype, side=side)
        h = g.__enter__()
        self._guards.setdefault(side, []).append((name, g))
        return h.ap()

    def free(self, *names):
        self.p.fence_all()
        for name in names:
            for side, stack in self._guards.items():
                if stack and stack[-1][0] == name:
                    _, g = stack.pop()
                    g.__exit__(None, None, None)
                    break
            else:
                raise AssertionError((name, {s: [n for n, _ in st[-3:]]
                                             for s, st in self._guards.items()}))

    def dma_in(self, dst, src, reads, writes, cast=False, tag=None):
        self.p.step("gq" if cast else "sq",
                    lambda e: e.dma_start(out=dst, in_=src),
                    reads=reads, writes=writes, ndma=1, tag=tag)

    def dma_out(self, dst, src, reads, writes=(), tag=None):
        self.p.step("sq", lambda e: e.dma_start(out=dst, in_=src),
                    reads=reads, writes=writes, ndma=1, tag=tag)

    PC_MATS = (("qkv", "l1f_wqkv", 8 * 3072), ("wo", "l1f_wo", 8 * 1024),
               ("w1a", "l1f_w1a", NKI * 1024), ("w1g", "l1f_w1g", NKI * 1024),
               ("w2", "l1f_w2", NKI * 1024))

    def setup_precast(self):
        PCW = 512
        self.stg = [self.alloc(f"pcstg{i}", [128, PCW], F32) for i in range(2)]
        self.stgb = [self.alloc(f"pcstgb{i}", [128, PCW], BF16) for i in range(2)]
        self.pcw = PCW
        self.wscr = {}
        self.pc_nch = {}
        self.pc_tasks = []
        for mn, srcn, cols in self.PC_MATS:
            self.wscr[mn] = self.dtmp("wscr_" + mn, [6, 128, cols], BF16)
            self.pc_nch[mn] = cols // PCW
        for li in range(6):
            for mn, srcn, cols in self.PC_MATS:
                src = self.din(srcn, [6, 128, cols])
                for ci, c0 in enumerate(range(0, cols, PCW)):
                    self.pc_tasks.append(
                        (src[li][:, c0:c0 + PCW],
                         self.wscr[mn][li][:, c0:c0 + PCW],
                         ("wsc", li, mn, ci)))
        self.pc_i = 0
        self.pc_j = 0
        self.pc_freed = False

    def _wsc_keys(self, li, mn):
        return [("wsc", li, mn, ci) for ci in range(self.pc_nch[mn])]

    def pump(self, n=1):
        for _ in range(n):
            if self.pc_j >= len(self.pc_tasks):
                return
            if self.pc_i < len(self.pc_tasks):
                src, dst, key = self.pc_tasks[self.pc_i]
                i = self.pc_i % 2
                stg = self.stg[i]
                self.p.step("sq", lambda e, src=src, stg=stg: e.dma_start(
                    out=stg[:, :], in_=src), reads=[], writes=[("pcstg", i)],
                    ndma=1, tag="pcin")
                self.pc_i += 1
            if self.pc_i - self.pc_j >= 2 or self.pc_i >= len(self.pc_tasks):
                src, dst, key = self.pc_tasks[self.pc_j]
                j = self.pc_j % 2
                stg, stgb = self.stg[j], self.stgb[j]
                self.p.step("act", lambda e, stg=stg, stgb=stgb: e.activation(
                    stgb[:, :], stg[:, :], AF.Copy),
                    reads=[("pcstg", j)], writes=[("pcstgb", j)])
                self.p.step("sq", lambda e, dst=dst, stgb=stgb: e.dma_start(
                    out=dst, in_=stgb[:, :]), reads=[("pcstgb", j)],
                    writes=[key], ndma=1, tag="pcout")
                self.pc_j += 1

    def pump_drain(self):
        while self.pc_j < len(self.pc_tasks):
            self.pump(1)
        if not self.pc_freed:
            self.free("pcstgb1", "pcstgb0", "pcstg1", "pcstg0")
            self.pc_freed = True

    def build(self):
        nc, p = self.nc, self.p
        T, TG, NTB = self.T, self.TG, self.NTB
        self.ps = [nc.alloc_psum_tensor(f"psb{i}", [128, 512], F32).ap()
                   for i in range(8)]
        self.psbf = [a.bitcast(BF16) for a in self.ps]
        self.ident = self.alloc("c_ident", [128, 128], BF16)
        self.p32 = self.alloc("c_p32", [128, 128], BF16)
        self.grpsel = self.alloc("c_grpsel", [128, 2], BF16)
        self.sel2 = self.alloc("c_sel2", [2, 128], F32)
        self.ones1 = self.alloc("c_ones1", [1, 128], F32)
        self.onescol = self.alloc("c_onescol", [128, 1], F32)
        self.onescolb = self.alloc("c_onescolb", [128, 1], BF16)
        self.rows = self.alloc("c_rows", [1, 2048], F32)
        self.eps1 = self.alloc("c_eps1", [1, 1], F32)
        self.eps2 = self.alloc("c_eps2", [2, 1], F32)
        self.attsc = self.alloc("c_attsc", [128, 32], F32)
        self.xhb = self.alloc("xhb", [128, NC * 512], BF16)
        self.sq2 = [self.alloc(f"sq2_{i}", [128, 512], F32) for i in range(2)]
        self.fsc = [self.alloc(f"fsc{i}", [128, 512], F32) for i in range(2)]
        self.xs = self.alloc("xs", [128, NC * TG], F32)
        self.xsb = self.xs.bitcast(BF16)
        self.spl = self.dtmp("spl", [NC, 128, T])
        if "l1f" in self.blocks:
            self.setup_precast()
        else:
            self.pc_tasks, self.pc_i, self.pc_j, self.pc_freed = [], 0, 0, True

        self.dma_in(self.ident[:, :], self.din("c_ident_in", [128, 128], BF16),
                    [], [("ident",)])
        self.dma_in(self.p32[:, :], self.din("c_p32_in", [128, 128], BF16),
                    [], [("p32",)])
        self.dma_in(self.grpsel[:, :], self.din("c_grpsel_in", [128, 2], BF16),
                    [], [("grpsel",)])
        self.dma_in(self.sel2[:, :], self.din("c_sel2_in", [2, 128], F32),
                    [], [("sel2",)])
        p.step("dve", lambda e: e.memset(self.ones1[:, :], 1.0), [], [("ones1",)])
        p.step("dve", lambda e: e.memset(self.onescol[:, :], 1.0), [], [("onescol",)])
        p.step("dve", lambda e: e.memset(self.onescolb[:, :], 1.0), [], [("onescolb",)])
        p.step("dve", lambda e: e.memset(self.eps1[:, :], 1e-5), [], [("eps1",)])
        p.step("dve", lambda e: e.memset(self.eps2[:, :], 1e-24), [], [("eps2",)])
        for c in range(NC):
            p.step("gp", lambda e, c=c: e.memset(
                self.xs[:, c * TG:c * TG + G], 0.0), [], [("xsg", c)])

        x_in = self.din("x_in", [self.nseq, NC, 128, T])
        y_out = self.dout("y_out", [self.nseq, NC, 128, T])

        for s in range(self.nseq):
            for c in range(NC):
                self.dma_in(self.xs[:, c * TG + G: c * TG + G + T], x_in[s, c],
                            [], [("xs", c, tb) for tb in range(NTB)], tag="ldx")
            for b in self.blocks:
                if b == "l1f" and not self.pc_freed:
                    self.pump_drain()
                getattr(self, "emit_" + b)(s)
            for c in range(NC):
                self.dma_out(y_out[s, c], self.xs[:, c * TG + G: c * TG + G + T],
                             [("xs", c, tb) for tb in range(NTB)], tag="stx")
        p.finish("sq")

    def xs_f(self, c, tb=None):
        base = c * self.TG + G
        if tb is None:
            return self.xs[:, base: base + self.T]
        return self.xs[:, base + tb * 512: base + tb * 512 + 512]

    def plane(self, j, lo=0, n=None):
        if n is None:
            n = self.T - lo
        base = j * self.TG
        return self.xsb[:, base + lo: base + lo + n]

    def spill_entry(self, s, tag, keep_planes=False):
        raise RuntimeError("inline in emitters")

    def ln_full(self, src_fn, keys_fn, NB=512, lane=0, src_bf=False,
                b_rb=6, b_mrb=7):
        p, ps = self.p, self.ps
        c0 = 0
        lane = 0
        rows = self.rows
        mu = rows[0:1, c0: c0 + NB]
        va = rows[0:1, c0 + 512: c0 + 512 + NB]
        ms = rows[0:1, c0 + 1024: c0 + 1024 + NB]
        rst = rows[0:1, c0 + 1536: c0 + 1536 + NB]
        r_mu, r_va = ("rw", lane, 0), ("rw", lane, 1)
        r_ms, r_rst = ("rw", lane, 2), ("rw", lane, 3)
        r_rb, r_mrb = ("ps", b_rb), ("ps", b_mrb)
        ones = self.onescolb if src_bf else self.onescol
        okey = ("onescolb",) if src_bf else ("onescol",)
        for c in range(NC):
            sq = self.sq2[c % 2]
            r_sq = ("sq2", c % 2)
            sap = src_fn(c)
            skey = keys_fn(c)
            p.step("act", lambda e, sap=sap, sq=sq: e.activation(
                sq[:, :NB], sap, AF.Square), reads=[skey], writes=[r_sq])
            def mm(e, c=c, sq=sq, sap=sap):
                e.matmul(ps[b_rb][0:1, :NB], ones[:, :], sap,
                         start=(c == 0), stop=(c == NC - 1), skip_group_check=True)
                return e.matmul(ps[b_mrb][0:1, :NB], self.onescol[:, :], sq[:, :NB],
                                start=(c == 0), stop=(c == NC - 1),
                                skip_group_check=True)
            p.step("pe", mm, reads=[skey, r_sq, okey, ("onescol",)],
                   writes=[r_rb, r_mrb])
        inv = 1.0 / D
        p.step("act", lambda e: e.activation(mu, ps[b_rb][0:1, :NB], AF.Copy,
                                             scale=inv), reads=[r_rb], writes=[r_mu])
        p.step("act", lambda e: e.activation(va, ps[b_mrb][0:1, :NB], AF.Copy,
                                             scale=inv), reads=[r_mrb], writes=[r_va])
        p.step("act", lambda e: e.activation(ms, mu, AF.Square),
               reads=[r_mu], writes=[r_ms])
        p.step("dve", lambda e: e.tensor_sub(va, va, ms),
               reads=[r_va, r_ms], writes=[r_va])
        p.step("act", lambda e: e.activation(rst, va, AF.Ln, bias=self.eps1[:, :]),
               reads=[r_va, ("eps1",)], writes=[r_rst])
        p.step("act", lambda e: e.activation(rst, rst, AF.Exp, scale=-0.5),
               reads=[r_rst], writes=[r_rst])
        p.step("dve", lambda e: e.tensor_mul(ms, mu, rst),
               reads=[r_mu, r_rst], writes=[r_ms])
        p.step("pe", lambda e: e.matmul(ps[b_rb][:, :NB], self.ones1[:, :], rst,
                                        start=True, stop=True),
               reads=[r_rst, ("ones1",)], writes=[r_rb])
        p.step("pe", lambda e: e.matmul(ps[b_mrb][:, :NB], self.ones1[:, :], ms,
                                        start=True, stop=True),
               reads=[r_ms, ("ones1",)], writes=[r_mrb])
        return r_rb, r_mrb

    def ln_apply_bf(self, src_fn, skeys_fn, dst_fn, dkeys_fn, NB=512,
                    b_rb=6, b_mrb=7):
        p, ps = self.p, self.ps
        for c in range(NC):
            fs = self.fsc[c % 2]
            r_fs = ("fsc", c % 2)
            sap, dap = src_fn(c), dst_fn(c)
            p.step("dve", lambda e, sap=sap, fs=fs: e.tensor_mul(
                fs[:, :NB], sap, ps[b_rb][:, :NB]),
                reads=[skeys_fn(c), ("ps", b_rb)], writes=[r_fs])
            p.step("dve", lambda e, dap=dap, fs=fs: e.tensor_sub(
                dap, fs[:, :NB], ps[b_mrb][:, :NB]),
                reads=[r_fs, ("ps", b_mrb)], writes=[dkeys_fn(c)])

    def plane_keys_of_f32(self, c, tb):
        TG2 = 2 * self.TG
        x0 = 2 * G + 1024 * tb
        x1 = x0 + 1024
        keys = []
        for (pl, a, b) in ((2 * c, x0, min(x1, self.TG)),
                           (2 * c + 1, max(x0, self.TG) - self.TG, x1 - self.TG)):
            if b > a:
                for t in range(a // 512, (b + 511) // 512):
                    if t * 512 < self.T:
                        keys.append(("pl", pl, t))
        return keys

    def f32_keys_of_plane(self, j, tb, s):
        c = j // 2
        b0 = (j % 2) * self.TG * 2 + 1024 * tb
        b1 = b0 + 1024
        keys = []
        if b0 < 2 * G:
            keys.append(("xsg", c))
        t0 = max(0, (b0 - 2 * G) // 1024)
        t1 = min(self.NTB - 1, (b1 - 2 * G - 1) // 1024)
        for t in range(t0, t1 + 1):
            keys.append(("xsv", s, c, t))
        return keys

    def emit_l0a(self, s):
        p, ps, psbf = self.p, self.ps, self.psbf
        T, TG, NTB = self.T, self.TG, self.NTB
        NKC = T // 128
        NQB = T // 256

        wdq_d = self.din("l0a_wdq", [128, 4 * 1024])
        wuqT_d = self.din("l0a_wuqT", [128, 4 * 1024])
        wuk_d = self.din("l0a_wuk", [128, 8 * 512])
        wuv_d = self.din("l0a_wuv", [128, 8 * 512])
        woT_d = self.din("l0a_woT", [128, 8 * 1024])
        wdkvT_d = self.din("l0a_wdkvT", [128, 8 * 512])

        cmask = self.alloc(f"l0a{s}_cmask", [128, 4 * 512], BF16)
        self.dma_in(cmask[:, :], self.din("c_cmask_in", [128, 4 * 512], BF16),
                    [], [("cmask",)])
        keff = self.alloc(f"l0a{s}_keff", [128, 8 * 512], BF16)
        vee = self.alloc(f"l0a{s}_vee", [128, 4 * 1024], BF16)
        veo = self.alloc(f"l0a{s}_veo", [128, 4 * 1024], BF16)

        A_sb = self.alloc(f"l0a{s}_A", [128, 8 * 1024], BF16, side="right")
        wdkvT = self.alloc(f"l0a{s}_wdkvT_sb", [128, 8 * 512], BF16, side="right")
        AT_sb = self.alloc(f"l0a{s}_AT", [128, 8 * 1024], BF16, side="right")
        wdq = self.alloc(f"l0a{s}_wdq_sb", [128, 4 * 1024], BF16, side="right")
        wuqT = self.alloc(f"l0a{s}_wuqT_sb", [128, 4 * 1024], BF16, side="right")
        self.dma_in(wdq[:, :], wdq_d, [], [("wdq",)], cast=True)
        self.dma_in(wuqT[:, :], wuqT_d, [], [("wuqT",)], cast=True)
        self.dma_in(wdkvT[:, :], wdkvT_d, [], [("wdkvT",)], cast=True)

        def prod(dst, dst_key, lhs, lhs_key, rhs, rhs_key, nk, Mn, Nn, kL, kR):
            for mt in range(Mn // 128):
                for nb in range(Nn // 512):
                    bank = (mt + nb) % 2
                    def mm(e, mt=mt, nb=nb, bank=bank):
                        r = None
                        for k in range(nk):
                            r = e.matmul(ps[bank][:, :],
                                         lhs[:, k * kL + mt * 128: k * kL + mt * 128 + 128],
                                         rhs[:, k * kR + nb * 512: k * kR + nb * 512 + 512],
                                         start=(k == 0), stop=(k == nk - 1))
                        return r
                    p.step("pe", mm, reads=[lhs_key, rhs_key], writes=[("ps", bank)])
                    p.step("act", lambda e, mt=mt, nb=nb, bank=bank: e.activation(
                        dst[:, mt * Nn + nb * 512: mt * Nn + nb * 512 + 512],
                        ps[bank][:, :], AF.Copy),
                        reads=[("ps", bank)], writes=[dst_key])

        prod(A_sb, ("A",), wdq, ("wdq",), wuqT, ("wuqT",), 4, 1024, 1024, 1024, 1024)
        prod(AT_sb, ("AT",), wuqT, ("wuqT",), wdq, ("wdq",), 4, 1024, 1024, 1024, 1024)
        self.free(f"l0a{s}_wuqT_sb", f"l0a{s}_wdq_sb")
        wuk = self.alloc(f"l0a{s}_wuk_sb", [128, 8 * 512], BF16, side="right")
        self.dma_in(wuk[:, :], wuk_d, [], [("wuk",)], cast=True)
        prod(keff, ("keff",), AT_sb, ("AT",), wuk, ("wuk",), 8, 1024, 512, 1024, 512)
        self.free(f"l0a{s}_wuk_sb", f"l0a{s}_AT")
        wuv = self.alloc(f"l0a{s}_wuv_sb", [128, 8 * 512], BF16, side="right")
        woT = self.alloc(f"l0a{s}_woT_sb", [128, 8 * 1024], BF16, side="right")
        self.dma_in(wuv[:, :], wuv_d, [], [("wuv",)], cast=True)
        self.dma_in(woT[:, :], woT_d, [], [("woT",)], cast=True)
        p.step("gp", lambda e: e.memset(vee[:, :], 0.0), [], [("vee",)])
        p.step("gp", lambda e: e.memset(veo[:, :], 0.0), [], [("veo",)])
        for lt in range(4):
            for hb in range(2):
                bank = (lt + hb) % 2
                def mm(e, lt=lt, hb=hb, bank=bank):
                    r = None
                    for k in range(8):
                        r = e.matmul(ps[bank][:, :],
                                     wuv[:, k * 512 + lt * 128: k * 512 + lt * 128 + 128],
                                     woT[:, k * 1024 + hb * 512: k * 1024 + hb * 512 + 512],
                                     start=(k == 0), stop=(k == 7))
                    return r
                p.step("pe", mm, reads=[("wuv",), ("woT",)], writes=[("ps", bank)])
                for par, dst, dkey in ((0, vee, ("vee",)), (1, veo, ("veo",))):
                    src3 = ps[bank][:, :].rearrange("p (b x) -> p b x", x=128)[:, :, par * 64:par * 64 + 64]
                    dst3 = dst[:, lt * 1024 + hb * 512: lt * 1024 + hb * 512 + 512] \
                        .rearrange("p (b x) -> p b x", x=128)[:, :, par * 64:par * 64 + 64]
                    p.step("dve", lambda e, srcp=src3, dstp=dst3: e.tensor_copy(dstp, srcp),
                           reads=[("ps", bank)], writes=[dkey])
        self.free(f"l0a{s}_woT_sb", f"l0a{s}_wuv_sb")

        ckvf = self.alloc(f"l0a{s}_ckvf", [128, 4 * T], BF16)
        ckvt = self.alloc(f"l0a{s}_ckvt", [128, NKC * 512], BF16)
        for tb in range(NTB):
            self.ln_full(lambda c, tb=tb: self.xs_f(c, tb),
                         lambda c, tb=tb: ("xs", c, tb), lane=tb % 2)
            self.ln_apply_bf(lambda c, tb=tb: self.xs_f(c, tb),
                             lambda c, tb=tb: ("xs", c, tb),
                             lambda c: self.xhb[:, c * 512:c * 512 + 512],
                             lambda c: ("xhb", c))
            for c in range(NC):
                self.dma_out(self.spl[c, :, tb * 512:(tb + 1) * 512], self.xs_f(c, tb),
                             reads=[("xs", c, tb)], writes=[("spl", s, c, tb)])
                p.claim([("xs", c, tb)], [("xsv", s, c, tb)])
            for lc in range(4):
                bank = lc % 2
                def mm(e, lc=lc, bank=bank):
                    r = None
                    for k in range(NC):
                        r = e.matmul(ps[bank][:, :],
                                     wdkvT[:, k * 512 + lc * 128: k * 512 + lc * 128 + 128],
                                     self.xhb[:, k * 512:(k + 1) * 512],
                                     start=(k == 0), stop=(k == NC - 1))
                    return r
                p.step("pe", mm, reads=[("wdkvT",)] + [("xhb", c) for c in range(NC)],
                       writes=[("ps", bank)])
                p.step("act", lambda e, lc=lc, bank=bank, tb=tb: e.activation(
                    ckvf[:, lc * T + tb * 512: lc * T + tb * 512 + 512],
                    ps[bank][:, :], AF.Copy),
                    reads=[("ps", bank)], writes=[("ckvf", lc, tb)])
            for lc in range(4):
                half = (lc % 2) * 512
                def tr(e, lc=lc, half=half, tb=tb):
                    r = None
                    for i in range(4):
                        tc = tb * 4 + i
                        r = e.transpose(psbf[4][:, half + i * 128: half + i * 128 + 128],
                                        ckvf[:, lc * T + tc * 128: lc * T + tc * 128 + 128],
                                        self.ident[:, :])
                    return r
                p.step("pe", tr, reads=[("ckvf", lc, tb), ("ident",)],
                       writes=[("ps", 4)])
                src3 = psbf[4][:, half: half + 512].rearrange("p (b x) -> p b x", x=128)
                ckvt_r = ckvt.rearrange("p (tc x) -> p tc x", x=512)
                dst3 = ckvt_r[:, tb * 4: tb * 4 + 4, lc * 128: lc * 128 + 128]
                p.step("dve", lambda e, s3=src3, d3=dst3: e.tensor_copy(d3, s3),
                       reads=[("ps", 4)], writes=[("ckvt", lc, tb)])
            for m in range(NC):
                bank = 2 + m % 2
                def mm(e, m=m, bank=bank):
                    r = None
                    for k in range(NC):
                        r = e.matmul(ps[bank][:, :],
                                     A_sb[:, k * 1024 + m * 128: k * 1024 + m * 128 + 128],
                                     self.xhb[:, k * 512:(k + 1) * 512],
                                     start=(k == 0), stop=(k == NC - 1))
                    return r
                p.step("pe", mm, reads=[("A",)] + [("xhb", c) for c in range(NC)],
                       writes=[("ps", bank)])
                qpl = self.plane(2 * m, tb * 512, 512)
                p.step("act", lambda e, qpl=qpl, bank=bank: e.activation(
                    qpl, ps[bank][:, :], AF.Copy),
                    reads=[("ps", bank)],
                    writes=[("pl", 2 * m, tb)] + self.f32_keys_of_plane(2 * m, tb, s))
        self.free(f"l0a{s}_wdkvT_sb", f"l0a{s}_A")

        qlat = [self.alloc(f"l0a{s}_qlat0", [128, 4 * T], BF16),
                self.alloc(f"l0a{s}_qlat1", [128, 4 * T], BF16)]
        pt = self.alloc(f"l0a{s}_pt", [128, NKC * 256], BF16, side="right")
        pbuf = self.alloc(f"l0a{s}_pbuf", [128, 8 * 512], BF16, side="right")
        pt_r = pt.rearrange("p (kc x) -> p kc x", x=256)
        att = self.attsc

        for ch in range(NC):
            for par in (0, 1):
                po = par * 64
                for lc in range(4):
                    for tk in range(NTB):
                        bank = lc % 2
                        lhs = keff[po:po + 64,
                                   ch * 512 + lc * 128: ch * 512 + lc * 128 + 128]
                        rhs = self.plane(2 * ch, tk * 512, 512)[po:po + 64, :]
                        dst = qlat[par][:, lc * T + tk * 512: lc * T + tk * 512 + 512]
                        p.step("pe", lambda e, lhs=lhs, rhs=rhs, bank=bank: e.matmul(
                            ps[bank][:, :], lhs, rhs, start=True, stop=True),
                            reads=[("keff",), ("pl", 2 * ch, tk)], writes=[("ps", bank)])
                        p.step("act", lambda e, dst=dst, bank=bank: e.activation(
                            dst, ps[bank][:, :], AF.Copy, scale=0.125),
                            reads=[("ps", bank)], writes=[("qlat", par, lc, tk)])
                def mk_l0a_iter(ch, par, qb):
                    nk5 = (qb + 2) // 2
                    nk1 = 2 * (qb + 1)

                    def S():
                        self.pump(8)
                        for qt in (0, 1):
                            qtg = 2 * qb + qt
                            vi = qtg - (nk5 - 1) * 4
                            b0 = qt * 4
                            a0 = qt * 16
                            for kc in range(nk5):
                                def mm(e, kc=kc, qtg=qtg, vi=vi, ql=qlat[par],
                                       b0=b0, masked=(kc == nk5 - 1)):
                                    r = None
                                    if masked:
                                        e.matmul(ps[b0 + kc][:, :], self.ident[:, :],
                                                 cmask[:, vi * 512:(vi + 1) * 512],
                                                 start=True, stop=True)
                                    for lc in range(4):
                                        r = e.matmul(
                                            ps[b0 + kc][:, :],
                                            ql[:, lc * T + qtg * 128: lc * T + qtg * 128 + 128],
                                            ckvf[:, lc * T + kc * 512: lc * T + kc * 512 + 512],
                                            start=(lc == 0 and not masked), stop=(lc == 3),
                                            skip_group_check=masked)
                                    return r
                                p.step("pe", mm,
                                       reads=[("qlat", par, lc, qtg // 4) for lc in range(4)]
                                       + [("ckvf", lc, kc) for lc in range(4)]
                                       + [("ident",), ("cmask",)],
                                       writes=[("ps", b0 + kc)])
                                p.step("dve", lambda e, kc=kc, b0=b0, a0=a0: e.reduce_max(
                                    att[:, a0 + kc:a0 + kc + 1], ps[b0 + kc][:, :],
                                    axis=AX.X, negate=True),
                                    reads=[("ps", b0 + kc)], writes=[("att", a0 + kc)])
                            p.step("dve", lambda e, nk5=nk5, a0=a0: e.tensor_reduce(
                                att[:, a0 + 8:a0 + 9], att[:, a0:a0 + nk5], AX.X, ALU.min),
                                reads=[("att", a0 + kc) for kc in range(nk5)],
                                writes=[("att", a0 + 8)])
                            for kc in range(nk5):
                                p.step("act", lambda e, kc=kc, b0=b0, a0=a0, qt=qt: e.activation(
                                    pbuf[:, qt * 2048 + kc * 512: qt * 2048 + kc * 512 + 512],
                                    ps[b0 + kc][:, :], AF.Exp,
                                    bias=att[:, a0 + 8:a0 + 9],
                                    accum_out=att[:, a0 + 9 + kc:a0 + 10 + kc]),
                                    reads=[("ps", b0 + kc), ("att", a0 + 8)],
                                    writes=[("pbuf", qt, kc), ("att", a0 + 9 + kc)])
                            p.step("dve", lambda e, nk5=nk5, a0=a0: e.reduce_sum(
                                att[:, a0 + 13:a0 + 14], att[:, a0 + 9:a0 + 9 + nk5], axis=AX.X),
                                reads=[("att", a0 + 9 + kc) for kc in range(nk5)],
                                writes=[("att", a0 + 13)])
                            p.step("dve", lambda e, a0=a0: e.reciprocal(
                                att[:, a0 + 14:a0 + 15], att[:, a0 + 13:a0 + 14]),
                                reads=[("att", a0 + 13)], writes=[("att", a0 + 14)])
                            p.step("dve", lambda e, nk5=nk5, a0=a0, qt=qt: e.tensor_scalar_mul(
                                pbuf[:, qt * 2048: qt * 2048 + nk5 * 512],
                                pbuf[:, qt * 2048: qt * 2048 + nk5 * 512],
                                att[:, a0 + 14:a0 + 15]),
                                reads=[("att", a0 + 14)] + [("pbuf", qt, kc) for kc in range(nk5)],
                                writes=[("pbuf", qt, kc) for kc in range(nk5)])

                    def Tst():
                        for qt in (0, 1):
                            tbk = qt * 4
                            for g0 in range(0, nk1, 8):
                                gn = min(8, nk1 - g0)
                                def tr(e, g0=g0, gn=gn, qt=qt, tbk=tbk):
                                    r = None
                                    for i in range(gn):
                                        r = e.transpose(
                                            psbf[tbk][:, i * 128:(i + 1) * 128],
                                            pbuf[:, qt * 2048 + (g0 + i) * 128:
                                                 qt * 2048 + (g0 + i) * 128 + 128],
                                            self.ident[:, :])
                                    return r
                                p.step("pe", tr,
                                       reads=[("pbuf", qt, kc) for kc in range(nk5)] + [("ident",)],
                                       writes=[("ps", tbk)])
                                src3 = psbf[tbk][:, 0:gn * 128].rearrange("p (b x) -> p b x", x=128)
                                dst3 = pt_r[:, g0:g0 + gn, qt * 128:qt * 128 + 128]
                                p.step("dve", lambda e, s3=src3, d3=dst3: e.tensor_copy(d3, s3),
                                       reads=[("ps", tbk)],
                                       writes=[("pt", kc, qt) for kc in range(g0, g0 + gn)])

                    def V():
                        def pv(e, qb=qb, nk1=nk1):
                            r = None
                            for kc in range(nk1):
                                for lc in range(4):
                                    r = e.matmul(
                                        ps[(1, 5)[lc // 2]][:, (lc % 2) * 256:(lc % 2) * 256 + 256],
                                        ckvt[:, kc * 512 + lc * 128: kc * 512 + lc * 128 + 128],
                                        pt[:, kc * 256:(kc + 1) * 256],
                                        start=(kc == 0 and lc % 2 == 0), stop=(kc == nk1 - 1),
                                        skip_group_check=True)
                            return r
                        p.step("pe", pv,
                               reads=[("pt", kc, qt) for kc in range(nk1) for qt in (0, 1)]
                               + [("ckvt", lc, tbx) for lc in range(4) for tbx in range((nk1 + 3) // 4)],
                               writes=[("ps", 1), ("ps", 5)])
                        for lc in range(4):
                            bank = (1, 5)[lc // 2]
                            dst = qlat[par][:, lc * T + qb * 256: lc * T + qb * 256 + 256]
                            srcp = ps[bank][:, (lc % 2) * 256:(lc % 2) * 256 + 256]
                            p.step("act", lambda e, dst=dst, srcp=srcp: e.activation(
                                dst, srcp, AF.Copy),
                                reads=[("ps", bank)], writes=[("qlat", par, lc, qb // 2)])
                    return S, Tst, V

                astages = [mk_l0a_iter(ch, par, qb) for qb in range(NQB)]
                for i in range(len(astages) + 2):
                    if i >= 2:
                        astages[i - 2][2]()
                    if 1 <= i <= len(astages):
                        astages[i - 1][1]()
                    if i < len(astages):
                        astages[i][0]()
            for tk in range(NTB):
                def ymm(e, ch=ch, tk=tk):
                    r = None
                    for par, ve_, ql in ((0, vee, qlat[0]), (1, veo, qlat[1])):
                        for lc in range(4):
                            r = e.matmul(
                                ps[7][:, :],
                                ve_[:, lc * 1024 + ch * 128: lc * 1024 + ch * 128 + 128],
                                ql[:, lc * T + tk * 512: lc * T + tk * 512 + 512],
                                start=(par == 0 and lc == 0), stop=(par == 1 and lc == 3),
                                skip_group_check=True)
                    return r
                p.step("pe", ymm,
                       reads=[("vee",), ("veo",)]
                       + [("qlat", par, lc, q2) for par in (0, 1) for lc in range(4)
                          for q2 in range(NTB)],
                       writes=[("ps", 7)])
                p.step("act", lambda e, ch=ch, tk=tk: e.activation(
                    self.plane(2 * ch, tk * 512, 512), ps[7][:, :], AF.Copy),
                    reads=[("ps", 7)], writes=[("pl", 2 * ch, tk)])
        if getattr(self, "dbg", False):
            dy = self.dout("dbg_y", [NC, 128, T], BF16)
            dl = self.dout("dbg_ylat", [2, 128, 4 * T], BF16)
            dp = self.dout("dbg_pt", [128, NKC * 256], BF16)
            for c in range(NC):
                self.dma_out(dy[c], self.plane(2 * c, 0, T),
                             reads=[("pl", 2 * c, tb) for tb in range(NTB)])
            for par in (0, 1):
                self.dma_out(dl[par], qlat[par][:, :],
                             reads=[("qlat", par, lc, tk) for lc in range(4)
                                    for tk in range(NTB)])
            self.dma_out(dp, pt[:, :],
                         reads=[("pt", kc, qt) for kc in range(NKC) for qt in (0, 1)])
        self.free(f"l0a{s}_pbuf", f"l0a{s}_pt", f"l0a{s}_qlat1", f"l0a{s}_qlat0")

        for tb in reversed(range(NTB)):
            self.ln_full(lambda c, tb=tb: self.plane(2 * c, tb * 512, 512),
                         lambda c, tb=tb: ("pl", 2 * c, tb), lane=tb % 2, src_bf=True)
            for c in range(NC):
                f0, f2 = self.fsc[0], self.sq2[0]
                ypl = self.plane(2 * c, tb * 512, 512)
                xap = self.xs_f(c, tb)
                p.step("dve", lambda e, ypl=ypl: e.tensor_mul(
                    f0[:, :], ypl, ps[6][:, :]),
                    reads=[("pl", 2 * c, tb), ("ps", 6)], writes=[("fsc", 0)])
                p.step("dve", lambda e: e.tensor_sub(f0[:, :], f0[:, :], ps[7][:, :]),
                       reads=[("fsc", 0), ("ps", 7)], writes=[("fsc", 0)])
                self.dma_in(f2[:, :], self.spl[c, :, tb * 512:(tb + 1) * 512],
                            reads=[("spl", s, c, tb)], writes=[("sq2", 0)])
                p.step("gp", lambda e, xap=xap: e.tensor_add(
                    xap, f0[:, :], f2[:, :]),
                    reads=[("fsc", 0), ("sq2", 0)],
                    writes=[("xs", c, tb)] + self.plane_keys_of_f32(c, tb))
        for c in range(NC):
            p.step("gp", lambda e, c=c: e.memset(self.xs[:, c * TG:c * TG + G], 0.0),
                   [], [("xsg", c), ("pl", 2 * c, 0)])
        self.free(f"l0a{s}_ckvt", f"l0a{s}_ckvf", f"l0a{s}_veo", f"l0a{s}_vee",
                  f"l0a{s}_keff", f"l0a{s}_cmask")
        self.p.fence_all()

    def emit_l0f(self, s):
        p, ps = self.p, self.ps
        T, TG, NTB = self.T, self.TG, self.NTB

        wnames = ["cn_w1", "cn_w2", "cn_w3", "cn_w4", "gc_w1", "gc_w2", "gc_w3"]
        shapes = {"cn_w1": 8 * 1024, "cn_w2": 8 * 512, "cn_w3": 4 * 256,
                  "cn_w4": 2 * 128, "gc_w1": 8 * 512, "gc_w2": 4 * 256,
                  "gc_w3": 2 * 128}
        W = {}
        for n in wnames:
            W[n] = self.alloc(f"l0f{s}_" + n, [128, shapes[n]], BF16)
            self.dma_in(W[n][:, :], self.din("l0f_" + n, [128, shapes[n]]),
                        [], [("w", n)], cast=True)
        B = {}
        for n, w_ in (("cn_b1", 8), ("cn_b2", 4), ("cn_b3", 2), ("gc_b1", 4),
                      ("gc_b2", 2)):
            B[n] = self.alloc(f"l0f{s}_" + n, [128, w_], F32)
            self.dma_in(B[n][:, :], self.din("l0f_" + n, [128, w_]), [], [("w", n)])
        for n in ("cn_b4", "gc_b3", "ar"):
            B[n] = self.alloc(f"l0f{s}_" + n, [1, 1], F32)
            self.dma_in(B[n][:, :], self.din("l0f_" + n, [1, 1]), [], [("w", n)])
        h1 = self.alloc(f"l0f{s}_h1", [128, 8 * 512], BF16)
        h2 = self.alloc(f"l0f{s}_h2", [128, 4 * 512], BF16)
        h3 = self.alloc(f"l0f{s}_h3", [128, 2 * 512], BF16)
        gsum = self.alloc(f"l0f{s}_gsum", [128, 8 * NTB], F32)
        gbb = self.alloc(f"l0f{s}_gbb", [128, 8], BF16)
        sc = self.alloc(f"l0f{s}_sc", [1, 8], F32)
        sc2 = self.alloc(f"l0f{s}_sc2", [1, 512], F32)

        for tb in range(NTB):
            self.ln_full(lambda c, tb=tb: self.xs_f(c, tb),
                         lambda c, tb=tb: ("xs", c, tb), lane=tb % 2)
            for c in range(NC):
                self.dma_out(self.spl[c, :, tb * 512:(tb + 1) * 512], self.xs_f(c, tb),
                             reads=[("xs", c, tb)], writes=[("spl", s, c, tb)])
            for c in range(NC):
                fs = self.fsc[c % 2]
                r_fs = ("fsc", c % 2)
                xap = self.xs_f(c, tb)
                p.step("dve", lambda e, xap=xap, fs=fs: e.tensor_mul(
                    fs[:, :], xap, ps[6][:, :]),
                    reads=[("xs", c, tb), ("ps", 6)], writes=[r_fs])
                p.step("dve", lambda e, xap=xap, fs=fs: e.tensor_sub(
                    xap, fs[:, :], ps[7][:, :]),
                    reads=[r_fs, ("ps", 7), ("spl", s, c, tb)],
                    writes=[("xs", c, tb)])
                p.step("dve", lambda e, xap=xap, c=c, tb=tb: e.reduce_sum(
                    gsum[:, c * NTB + tb: c * NTB + tb + 1], xap, axis=AX.X),
                    reads=[("xs", c, tb)], writes=[("gsum", c, tb)])

        for c in range(NC):
            if NTB > 1:
                p.step("dve", lambda e, c=c: e.reduce_sum(
                    gsum[:, c * NTB: c * NTB + 1], gsum[:, c * NTB: (c + 1) * NTB],
                    axis=AX.X),
                    reads=[("gsum", c, tb) for tb in range(NTB)], writes=[("gsum", c)])
            p.step("dve", lambda e, c=c: e.tensor_scalar_mul(
                gbb[:, c:c + 1], gsum[:, c * NTB: c * NTB + 1], 1.0 / T),
                reads=[("gsum", c)] + ([] if NTB > 1 else [("gsum", c, 0)]),
                writes=[("gbb", c)])
        def gmm1(e):
            r = None
            for m in range(4):
                for k in range(NC):
                    r = e.matmul(ps[0][:, m:m + 1],
                                 W["gc_w1"][:, k * 512 + m * 128: k * 512 + m * 128 + 128],
                                 gbb[:, k:k + 1], start=(m == 0 and k == 0),
                                 stop=(m == 3 and k == NC - 1), skip_group_check=True)
            return r
        p.step("pe", gmm1, reads=[("w", "gc_w1")] + [("gbb", c) for c in range(NC)],
               writes=[("ps", 0)])
        for m in range(4):
            p.step("act", lambda e, m=m: e.activation(
                gbb[:, m:m + 1], ps[0][:, m:m + 1], AF.Gelu,
                bias=B["gc_b1"][:, m:m + 1]),
                reads=[("ps", 0), ("w", "gc_b1")], writes=[("g1", m)])
        def gmm2(e):
            r = None
            for m in range(2):
                for k in range(4):
                    r = e.matmul(ps[1][:, m:m + 1],
                                 W["gc_w2"][:, k * 256 + m * 128: k * 256 + m * 128 + 128],
                                 gbb[:, k:k + 1], start=(m == 0 and k == 0),
                                 stop=(m == 1 and k == 3), skip_group_check=True)
            return r
        p.step("pe", gmm2, reads=[("w", "gc_w2")] + [("g1", m) for m in range(4)],
               writes=[("ps", 1)])
        for m in range(2):
            p.step("act", lambda e, m=m: e.activation(
                gbb[:, m:m + 1], ps[1][:, m:m + 1], AF.Gelu,
                bias=B["gc_b2"][:, m:m + 1]),
                reads=[("ps", 1), ("w", "gc_b2")], writes=[("g2", m)])
        def gmm3(e):
            r = None
            for k in range(2):
                r = e.matmul(ps[2][:, 0:1], W["gc_w3"][:, k * 128:(k + 1) * 128],
                             gbb[:, k:k + 1], start=(k == 0), stop=(k == 1))
            return r
        p.step("pe", gmm3, reads=[("w", "gc_w3")] + [("g2", m) for m in range(2)],
               writes=[("ps", 2)])
        p.step("act", lambda e: e.activation(sc[:, 0:1], ps[2][0:1, 0:1], AF.Sigmoid,
                                             bias=B["gc_b3"][:, :]),
               reads=[("ps", 2), ("w", "gc_b3")], writes=[("sc", 0)])
        p.step("dve", lambda e: e.tensor_mul(sc[:, 1:2], sc[:, 0:1], B["ar"][:, :]),
               reads=[("sc", 0), ("w", "ar")], writes=[("sc", 1)])
        p.step("dve", lambda e: e.tensor_scalar(sc[:, 2:3], sc[:, 1:2], 1.4, 0.1,
                                                ALU.mult, ALU.add),
               reads=[("sc", 1)], writes=[("sc", 2)])
        p.step("dve", lambda e: e.tensor_sub(sc[:, 2:3], sc[:, 2:3], B["ar"][:, :]),
               reads=[("sc", 2), ("w", "ar")], writes=[("sc", 2)])
        p.step("dve", lambda e: e.tensor_scalar_mul(sc[:, 3:4], B["ar"][:, :], 0.6),
               reads=[("w", "ar")], writes=[("sc", 3)])

        PAIRS = ((0, 4, 1.0), (1, 5, 1.0), (2, 6, -1.0), (3, 7, -1.0))
        for r_ in range(3):
            for tb in range(NTB):
                self.pump(4)
                for c in range(NC):
                    xap = self.xs_f(c, tb)
                    p.step("dve", lambda e, xap=xap, c=c: e.tensor_copy(
                        self.xhb[:, c * 512:(c + 1) * 512], xap),
                        reads=[("xs", c, tb)], writes=[("xhb", c)])
                for m in range(NC):
                    bank = m % 2
                    def mm(e, m=m, bank=bank):
                        r = None
                        for k in range(NC):
                            r = e.matmul(ps[bank][:, :],
                                         W["cn_w1"][:, k * 1024 + m * 128: k * 1024 + m * 128 + 128],
                                         self.xhb[:, k * 512:(k + 1) * 512],
                                         start=(k == 0), stop=(k == NC - 1))
                        return r
                    p.step("pe", mm, reads=[("w", "cn_w1")] + [("xhb", c) for c in range(NC)],
                           writes=[("ps", bank)])
                    p.step("act", lambda e, m=m, bank=bank: e.activation(
                        h1[:, m * 512:(m + 1) * 512], ps[bank][:, :], AF.Gelu,
                        bias=B["cn_b1"][:, m:m + 1]),
                        reads=[("ps", bank), ("w", "cn_b1")], writes=[("h1", m)])
                for m in range(4):
                    bank = 2 + m % 2
                    def mm(e, m=m, bank=bank):
                        r = None
                        for k in range(NC):
                            r = e.matmul(ps[bank][:, :],
                                         W["cn_w2"][:, k * 512 + m * 128: k * 512 + m * 128 + 128],
                                         h1[:, k * 512:(k + 1) * 512],
                                         start=(k == 0), stop=(k == NC - 1))
                        return r
                    p.step("pe", mm, reads=[("w", "cn_w2")] + [("h1", k) for k in range(NC)],
                           writes=[("ps", bank)])
                    p.step("act", lambda e, m=m, bank=bank: e.activation(
                        h2[:, m * 512:(m + 1) * 512], ps[bank][:, :], AF.Gelu,
                        bias=B["cn_b2"][:, m:m + 1]),
                        reads=[("ps", bank), ("w", "cn_b2")], writes=[("h2", m)])
                for m in range(2):
                    bank = m % 2
                    def mm(e, m=m, bank=bank):
                        r = None
                        for k in range(4):
                            r = e.matmul(ps[bank][:, :],
                                         W["cn_w3"][:, k * 256 + m * 128: k * 256 + m * 128 + 128],
                                         h2[:, k * 512:(k + 1) * 512],
                                         start=(k == 0), stop=(k == 3))
                        return r
                    p.step("pe", mm, reads=[("w", "cn_w3")] + [("h2", k) for k in range(4)],
                           writes=[("ps", bank)])
                    p.step("act", lambda e, m=m, bank=bank: e.activation(
                        h3[:, m * 512:(m + 1) * 512], ps[bank][:, :], AF.Gelu,
                        bias=B["cn_b3"][:, m:m + 1]),
                        reads=[("ps", bank), ("w", "cn_b3")], writes=[("h3", m)])
                def mm4(e):
                    r = None
                    for k in range(2):
                        r = e.matmul(ps[4][:, :], W["cn_w4"][:, k * 128:(k + 1) * 128],
                                     h3[:, k * 512:(k + 1) * 512],
                                     start=(k == 0), stop=(k == 1))
                    return r
                p.step("pe", mm4, reads=[("w", "cn_w4")] + [("h3", k) for k in range(2)],
                       writes=[("ps", 4)])
                p.step("act", lambda e: e.activation(
                    sc2[:, :], ps[4][0:1, :], AF.Sigmoid, bias=B["cn_b4"][:, :]),
                    reads=[("ps", 4), ("w", "cn_b4")], writes=[("sc2",)])
                p.step("dve", lambda e: e.tensor_scalar(
                    sc2[:, :], sc2[:, :], sc[:, 3:4], sc[:, 2:3], ALU.mult, ALU.add),
                    reads=[("sc2",), ("sc", 2), ("sc", 3)], writes=[("sc2",)])
                p.step("pe", lambda e: e.matmul(ps[5][:, :], self.ones1[:, :],
                                                sc2[:, :], start=True, stop=True),
                       reads=[("sc2",), ("ones1",)], writes=[("ps", 5)])
                for (a, b, sa) in PAIRS:
                    sv = self.fsc[1]
                    f0 = self.fsc[0]
                    xa, xb = self.xs_f(a, tb), self.xs_f(b, tb)
                    p.step("gp", lambda e, xa=xa: e.tensor_copy(sv[:, :], xa),
                           reads=[("xs", a, tb)], writes=[("fsc", 1)])
                    p.step("dve", lambda e, xb=xb: e.tensor_mul(f0[:, :], xb, ps[5][:, :]),
                           reads=[("xs", b, tb), ("ps", 5)], writes=[("fsc", 0)])
                    p.step("gp", lambda e, xa=xa, sa=sa: e.tensor_tensor(
                        xa, xa, f0[:, :], ALU.add if sa > 0 else ALU.subtract),
                        reads=[("fsc", 0), ("xs", a, tb)], writes=[("xs", a, tb)])
                    p.step("dve", lambda e: e.tensor_mul(f0[:, :], sv[:, :], ps[5][:, :]),
                           reads=[("fsc", 1), ("ps", 5)], writes=[("fsc", 0)])
                    p.step("gp", lambda e, xb=xb, sa=sa: e.tensor_tensor(
                        xb, xb, f0[:, :], ALU.subtract if sa > 0 else ALU.add),
                        reads=[("fsc", 0), ("xs", b, tb)], writes=[("xs", b, tb)])

        for tb in range(NTB):
            self.ln_full(lambda c, tb=tb: self.xs_f(c, tb),
                         lambda c, tb=tb: ("xs", c, tb), lane=tb % 2)
            for c in range(NC):
                f0, f2 = self.fsc[0], self.sq2[0]
                xap = self.xs_f(c, tb)
                p.step("dve", lambda e, xap=xap: e.tensor_mul(
                    f0[:, :], xap, ps[6][:, :]),
                    reads=[("xs", c, tb), ("ps", 6)], writes=[("fsc", 0)])
                p.step("dve", lambda e: e.tensor_sub(f0[:, :], f0[:, :], ps[7][:, :]),
                       reads=[("fsc", 0), ("ps", 7)], writes=[("fsc", 0)])
                self.dma_in(f2[:, :], self.spl[c, :, tb * 512:(tb + 1) * 512],
                            reads=[("spl", s, c, tb)], writes=[("sq2", 0)])
                p.step("gp", lambda e, xap=xap: e.tensor_add(
                    xap, f0[:, :], f2[:, :]),
                    reads=[("fsc", 0), ("sq2", 0)], writes=[("xs", c, tb)])
        self.free(*[f"l0f{s}_" + n for n in ("sc2", "sc", "gbb", "gsum", "h3", "h2", "h1", "ar", "gc_b3", "cn_b4", "gc_b2", "gc_b1", "cn_b3", "cn_b2", "cn_b1", "gc_w3", "gc_w2", "gc_w1", "cn_w4", "cn_w3", "cn_w2", "cn_w1")])
        self.p.fence_all()

    def emit_l1a(self, s):
        p, ps, psbf = self.p, self.ps, self.psbf
        T, TG, NTB = self.T, self.TG, self.NTB
        NQB = T // 256
        NQT = T // 128
        wqkv_d = self.din("l1a_wqkv", [128, 8 * 3072])
        wqkv_r = wqkv_d.rearrange("p (c o) -> p c o", o=3072)
        wproj_d = self.din("l1a_wproj", [128, 8 * 1024])
        bproj_d = self.din("l1a_bproj", [128, 8])
        g_d = self.din("l1a_g", [128, 8])
        b_d = self.din("l1a_b", [128, 8])

        maskd = self.alloc(f"l1a{s}_maskd", [128, 128], BF16)
        self.dma_in(maskd[:, :],
                    self.din("c_maskt_in", [128, 3 * 128], BF16)[:, 256:384],
                    [], [("maskd",)])
        kf = self.alloc(f"l1a{s}_kf", [128, 8 * T], BF16)
        vtm = self.alloc(f"l1a{s}_vtm", [128, (T // 128) * 1040], BF16)
        vtm5 = vtm.rearrange("p (t c g d) -> p t c g d", c=8, g=2, d=65)
        vtm65 = vtm.rearrange("p (t d) -> p t d", d=65)
        p.step("dve", lambda e: e.memset(vtm65[:, :, 64:65], 1.0),
               [], [("vtm1",)])
        wq = [self.alloc(f"l1a{s}_wq{i}", [128, 8 * 512], BF16) for i in range(2)]
        vf = self.alloc(f"l1a{s}_vf", [128, 512], BF16)

        for tb in range(NTB):
            self.ln_full(lambda c, tb=tb: self.xs_f(c, tb),
                         lambda c, tb=tb: ("xs", c, tb), lane=tb % 2)
            self.ln_apply_bf(lambda c, tb=tb: self.xs_f(c, tb),
                             lambda c, tb=tb: ("xs", c, tb),
                             lambda c: self.xhb[:, c * 512:c * 512 + 512],
                             lambda c: ("xhb", c))
            for c in range(NC):
                self.dma_out(self.spl[c, :, tb * 512:(tb + 1) * 512], self.xs_f(c, tb),
                             reads=[("xs", c, tb)], writes=[("spl", s, c, tb)])
                p.claim([("xs", c, tb)], [("xsv", s, c, tb)])
            for og in range(6):
                wbuf = wq[og % 2]
                wkeys = [("wq", og % 2)] + ([("wo",)] if og % 2 == 0 else [])
                self.dma_in(wbuf[:, :], wqkv_r[:, :, og * 512:(og + 1) * 512],
                            reads=[], writes=wkeys, cast=True)
                for ot in range(4):
                    o = og * 4 + ot
                    bank = ot % 2
                    def mm(e, ot=ot, bank=bank, wbuf=wbuf):
                        r = None
                        for k in range(NC):
                            r = e.matmul(ps[bank][:, :],
                                         wbuf[:, k * 512 + ot * 128: k * 512 + ot * 128 + 128],
                                         self.xhb[:, k * 512:(k + 1) * 512],
                                         start=(k == 0), stop=(k == NC - 1))
                        return r
                    p.step("pe", mm,
                           reads=[("wq", og % 2)] + [("xhb", c) for c in range(NC)],
                           writes=[("ps", bank)])
                    if o < 8:
                        qpl = self.plane(2 * o, tb * 512, 512)
                        p.step("act", lambda e, qpl=qpl, bank=bank: e.activation(
                            qpl, ps[bank][:, :], AF.Copy, scale=0.125),
                            reads=[("ps", bank)],
                            writes=[("pl", 2 * o, tb)] + self.f32_keys_of_plane(2 * o, tb, s))
                    elif o < 16:
                        m = o - 8
                        p.step("act", lambda e, m=m, bank=bank, tb=tb: e.activation(
                            kf[:, m * T + tb * 512: m * T + tb * 512 + 512],
                            ps[bank][:, :], AF.Copy),
                            reads=[("ps", bank)], writes=[("kf", m, tb)])
                    else:
                        m = o - 16
                        p.step("act", lambda e, bank=bank: e.activation(
                            vf[:, :], ps[bank][:, :], AF.Copy),
                            reads=[("ps", bank)], writes=[("vf",)])
                        def tr(e):
                            r = None
                            for i in range(4):
                                r = e.transpose(psbf[4][:, i * 128:(i + 1) * 128],
                                                vf[:, i * 128:(i + 1) * 128],
                                                self.ident[:, :])
                            return r
                        p.step("pe", tr, reads=[("vf",), ("ident",)], writes=[("ps", 4)])
                        src4 = psbf[4][:, 0:512].rearrange(
                            "p (b g x) -> p b g x", g=2, x=64)
                        dst4 = vtm5[:, tb * 4: tb * 4 + 4, m, :, 0:64]
                        p.step("dve", lambda e, s4=src4, d4=dst4: e.tensor_copy(d4, s4),
                               reads=[("ps", 4)], writes=[("vtm", m, tb)])
        self.free(f"l1a{s}_vf")

        pt = self.alloc(f"l1a{s}_pt", [128, NQT * 256], BF16)
        pbuf = self.alloc(f"l1a{s}_pbuf", [128, 8 * 512], BF16)
        ytp = self.alloc(f"l1a{s}_ytp", [128, T], BF16)
        pt_r = pt.rearrange("p (kc x) -> p kc x", x=256)
        att = self.attsc
        def mk_l1a_iter(ch, par, qb):
            po = par * 64
            nk5 = (qb + 2) // 2

            def S():
                self.pump(8)
                for qt in (0, 1):
                    qtg = 2 * qb + qt
                    b0 = qt * 4
                    a0 = qt * 16
                    qsl = self.plane(2 * ch, qtg * 128, 128)[po:po + 64, :]
                    for kc in range(nk5):
                        def mm(e, kc=kc, qsl=qsl, b0=b0):
                            return e.matmul(
                                ps[b0 + kc][:, :], qsl,
                                kf[po:po + 64,
                                   ch * T + kc * 512: ch * T + kc * 512 + 512],
                                start=True, stop=True)
                        p.step("pe", mm,
                               reads=[("pl", 2 * ch, qtg // 4), ("kf", ch, kc)],
                               writes=[("ps", b0 + kc)])
                        p.step("dve", lambda e, kc=kc, b0=b0, a0=a0: e.reduce_max(
                            att[:, a0 + kc:a0 + kc + 1], ps[b0 + kc][:, :],
                            axis=AX.X, negate=True),
                            reads=[("ps", b0 + kc)], writes=[("att", a0 + kc)])
                    p.step("dve", lambda e, a0=a0: e.tensor_reduce(
                        att[:, a0 + 8:a0 + 9], att[:, a0:a0 + nk5], AX.X, ALU.min),
                        reads=[("att", a0 + kc) for kc in range(nk5)],
                        writes=[("att", a0 + 8)])
                    for kc in range(nk5):
                        p.step("act", lambda e, kc=kc, b0=b0, a0=a0, qt=qt: e.activation(
                            pbuf[:, qt * 2048 + kc * 512: qt * 2048 + kc * 512 + 512],
                            ps[b0 + kc][:, :], AF.Exp,
                            bias=att[:, a0 + 8:a0 + 9]),
                            reads=[("ps", b0 + kc), ("att", a0 + 8)],
                            writes=[("pbuf", qt, kc)])

            def Tst():
                for qt in (0, 1):
                    qtg = 2 * qb + qt
                    ntile = qtg + 1
                    tbk = qt * 4
                    for g0 in range(0, ntile, 8):
                        gn = min(8, ntile - g0)
                        def tr(e, g0=g0, gn=gn, qt=qt, tbk=tbk):
                            r = None
                            for i in range(gn):
                                r = e.transpose(
                                    psbf[tbk][:, i * 128:(i + 1) * 128],
                                    pbuf[:, qt * 2048 + (g0 + i) * 128:
                                         qt * 2048 + (g0 + i) * 128 + 128],
                                    self.ident[:, :])
                            return r
                        p.step("pe", tr,
                               reads=[("pbuf", qt, kc) for kc in range(nk5)]
                               + [("ident",)],
                               writes=[("ps", tbk)])
                        dn = gn - (1 if g0 + gn == ntile else 0)
                        if dn > 0:
                            src3 = psbf[tbk][:, 0:dn * 128].rearrange(
                                "p (b x) -> p b x", x=128)
                            dst3 = pt_r[:, g0:g0 + dn, qt * 128:qt * 128 + 128]
                            p.step("dve", lambda e, s3=src3, d3=dst3: e.tensor_copy(
                                d3, s3), reads=[("ps", tbk)],
                                writes=[("pt", kc, qt) for kc in range(g0, g0 + dn)])
                        if g0 + gn == ntile:
                            srcd = psbf[tbk][:, (gn - 1) * 128: gn * 128]
                            dstd = pt_r[:, ntile - 1, qt * 128:qt * 128 + 128]
                            p.step("dve", lambda e, srcd=srcd, dstd=dstd: e.tensor_tensor(
                                dstd, srcd, maskd[:, :], ALU.mult),
                                reads=[("ps", tbk), ("maskd",)],
                                writes=[("pt", ntile - 1, qt)])

            def V():
                for qt in (0, 1):
                    qtg = 2 * qb + qt
                    ntile = qtg + 1
                    a0 = qt * 16
                    bv = (1, 5)[qt]
                    def pv(e, ntile=ntile, qt=qt, bv=bv):
                        r = None
                        for kc in range(ntile):
                            r = e.matmul(
                                ps[bv][:, 0:65],
                                pt[:, kc * 256 + qt * 128: kc * 256 + qt * 128 + 128],
                                vtm[:, kc * 1040 + ch * 130 + par * 65:
                                    kc * 1040 + ch * 130 + par * 65 + 65],
                                start=(kc == 0), stop=(kc == ntile - 1),
                                skip_group_check=True)
                        return r
                    p.step("pe", pv,
                           reads=[("pt", kc, qt) for kc in range(ntile)]
                           + [("vtm", ch, tbx) for tbx in range((ntile + 3) // 4)]
                           + [("vtm1",)],
                           writes=[("ps", bv)])
                    p.step("dve", lambda e, bv=bv, a0=a0: e.reciprocal(
                        att[:, a0 + 14:a0 + 15], ps[bv][:, 64:65]),
                        reads=[("ps", bv)], writes=[("att", a0 + 14)])
                    dst = ytp[:, qtg * 128 + po: qtg * 128 + po + 64]
                    p.step("act", lambda e, dst=dst, bv=bv, a0=a0: e.activation(
                        dst, ps[bv][:, 0:64], AF.Copy,
                        scale=att[:, a0 + 14:a0 + 15]),
                        reads=[("ps", bv), ("att", a0 + 14)],
                        writes=[("ytp", qtg, par)])
            return S, Tst, V

        def mk_l1a_post(ch):
            def post():
                for tk in range(NTB):
                    def tr(e, tk=tk):
                        r = None
                        for i in range(4):
                            qtile = tk * 4 + i
                            r = e.transpose(psbf[4][:, i * 128:(i + 1) * 128],
                                            ytp[:, qtile * 128:(qtile + 1) * 128],
                                            self.ident[:, :])
                        return r
                    p.step("pe", tr,
                           reads=[("ytp", tk * 4 + i, par)
                                  for i in range(4) for par in (0, 1)]
                           + [("ident",)],
                           writes=[("ps", 4)])
                    src3 = psbf[4][:, 0:512].rearrange("p (b x) -> p b x", x=128)
                    ypl = self.plane(2 * ch, tk * 512, 512).rearrange(
                        "p (b x) -> p b x", x=128)
                    p.step("dve", lambda e, s3=src3, d3=ypl: e.tensor_copy(d3, s3),
                           reads=[("ps", 4)], writes=[("pl", 2 * ch, tk)])
            return post

        lstages = []
        for ch in range(NC):
            for par in (0, 1):
                for qb in range(NQB):
                    lstages.append(mk_l1a_iter(ch, par, qb)
                                   + ((mk_l1a_post(ch),)
                                      if (par == 1 and qb == NQB - 1) else (None,)))
        for i in range(len(lstages) + 2):
            if i >= 2:
                lstages[i - 2][2]()
                if lstages[i - 2][3] is not None:
                    lstages[i - 2][3]()
            if 1 <= i <= len(lstages):
                lstages[i - 1][1]()
            if i < len(lstages):
                lstages[i][0]()
        self.free(f"l1a{s}_ytp", f"l1a{s}_pbuf", f"l1a{s}_pt",
                  f"l1a{s}_wq1", f"l1a{s}_wq0", f"l1a{s}_vtm", f"l1a{s}_kf")

        zb = self.alloc(f"l1a{s}_zb", [128, 8 * 512], F32)
        xrl = self.alloc(f"l1a{s}_xrl", [128, 8 * 512], F32)
        wp = self.alloc(f"l1a{s}_wp", [128, 8 * 1024], BF16)
        gb = self.alloc(f"l1a{s}_gb", [128, 8], F32)
        bb = self.alloc(f"l1a{s}_bb", [128, 8], F32)
        bpj = self.alloc(f"l1a{s}_bpj", [128, 8], F32)
        self.dma_in(wp[:, :], wproj_d, [], [("wp",)], cast=True)
        self.dma_in(gb[:, :], g_d, [], [("lng",)])
        self.dma_in(bb[:, :], b_d, [], [("lnb",)])
        self.dma_in(bpj[:, :], bproj_d, [], [("bpj",)])
        for tb in reversed(range(NTB)):
            for c in range(NC):
                self.dma_in(xrl[:, c * 512:(c + 1) * 512],
                            self.spl[c, :, tb * 512:(tb + 1) * 512],
                            reads=[("spl", s, c, tb)], writes=[("xrl", c)])
            self.ln_full(lambda c: xrl[:, c * 512:(c + 1) * 512],
                         lambda c: ("xrl", c), lane=tb % 2)
            for c in range(NC):
                fs = self.fsc[c % 2]
                xap = xrl[:, c * 512:(c + 1) * 512]
                p.step("dve", lambda e, xap=xap, fs=fs: e.tensor_mul(
                    fs[:, :], xap, ps[6][:, :]),
                    reads=[("xrl", c), ("ps", 6)], writes=[("fsc", c % 2)])
                p.step("dve", lambda e, xap=xap, fs=fs: e.tensor_sub(
                    xap, fs[:, :], ps[7][:, :]),
                    reads=[("fsc", c % 2), ("ps", 7)], writes=[("xrl", c)])
            for o in range(NC):
                bank = o % 4
                def mm(e, o=o, bank=bank, tb=tb):
                    r = None
                    for k in range(NC):
                        r = e.matmul(ps[bank][:, :],
                                     wp[:, k * 1024 + o * 128: k * 1024 + o * 128 + 128],
                                     self.plane(2 * k, tb * 512, 512),
                                     start=(k == 0), stop=(k == NC - 1))
                    return r
                p.step("pe", mm,
                       reads=[("wp",)] + [("pl", 2 * k, tb) for k in range(NC)],
                       writes=[("ps", bank)])
                p.step("dve", lambda e, o=o, bank=bank: e.scalar_tensor_tensor(
                    zb[:, o * 512:(o + 1) * 512], ps[bank][:, :], bpj[:, o:o + 1],
                    xrl[:, o * 512:(o + 1) * 512], ALU.add, ALU.add),
                    reads=[("ps", bank), ("bpj",), ("xrl", o)], writes=[("zb", o)])
            self.ln_full(lambda c: zb[:, c * 512:(c + 1) * 512],
                         lambda c: ("zb", c), lane=tb % 2)
            for c in range(NC):
                fs = self.fsc[c % 2]
                zap = zb[:, c * 512:(c + 1) * 512]
                p.step("dve", lambda e, zap=zap, fs=fs: e.tensor_mul(
                    fs[:, :], zap, ps[6][:, :]),
                    reads=[("zb", c), ("ps", 6)], writes=[("fsc", c % 2)])
                p.step("dve", lambda e, zap=zap, fs=fs: e.tensor_sub(
                    fs[:, :], fs[:, :], ps[7][:, :]),
                    reads=[("fsc", c % 2), ("ps", 7)], writes=[("fsc", c % 2)])
                p.step("dve", lambda e, zap=zap, fs=fs, c=c: e.tensor_scalar(
                    zap, fs[:, :], gb[:, c:c + 1], bb[:, c:c + 1], ALU.mult, ALU.add),
                    reads=[("fsc", c % 2), ("lng",), ("lnb",)], writes=[("zb", c)])
            self.ln_full(lambda c: zb[:, c * 512:(c + 1) * 512],
                         lambda c: ("zb", c), lane=tb % 2)
            for c in range(NC):
                f0, f2 = self.fsc[0], self.sq2[0]
                zap = zb[:, c * 512:(c + 1) * 512]
                xap = self.xs_f(c, tb)
                p.step("dve", lambda e, zap=zap: e.tensor_mul(
                    f0[:, :], zap, ps[6][:, :]),
                    reads=[("zb", c), ("ps", 6)], writes=[("fsc", 0)])
                p.step("dve", lambda e: e.tensor_sub(f0[:, :], f0[:, :], ps[7][:, :]),
                       reads=[("fsc", 0), ("ps", 7)], writes=[("fsc", 0)])
                self.dma_in(f2[:, :], self.spl[c, :, tb * 512:(tb + 1) * 512],
                            reads=[("spl", s, c, tb)], writes=[("sq2", 0)])
                p.step("gp", lambda e, xap=xap: e.tensor_add(
                    xap, f0[:, :], f2[:, :]),
                    reads=[("fsc", 0), ("sq2", 0)],
                    writes=[("xs", c, tb)] + self.plane_keys_of_f32(c, tb))
        for c in range(NC):
            p.step("gp", lambda e, c=c: e.memset(self.xs[:, c * TG:c * TG + G], 0.0),
                   [], [("xsg", c), ("pl", 2 * c, 0)])
        self.free(f"l1a{s}_bpj", f"l1a{s}_bb", f"l1a{s}_gb", f"l1a{s}_wp",
                  f"l1a{s}_xrl", f"l1a{s}_zb", f"l1a{s}_maskd")
        self.p.fence_all()

    def emit_l1f(self, s):
        p, ps, psbf = self.p, self.ps, self.psbf
        T, TG, NTB = self.T, self.TG, self.NTB
        wqkv_d = self.din("l1f_wqkv", [6, 128, 8 * 3072])
        wo_d = self.din("l1f_wo", [6, 128, 8 * 1024])
        qsks_d = self.din("l1f_qsks", [128, 6 * 4])
        g_d = self.din("l1f_g", [6, 128, 8])
        b_d = self.din("l1f_b", [6, 128, 8])
        w1a_d = self.din("l1f_w1a", [6, 128, NKI * 1024])
        w1g_d = self.din("l1f_w1g", [6, 128, NKI * 1024])
        w2_d = self.din("l1f_w2", [6, 128, NKI * 1024])
        b1a_d = self.din("l1f_b1a", [6, 128, NKI])
        b1g_d = self.din("l1f_b1g", [6, 128, NKI])
        b2_d = self.din("l1f_b2", [6, 128, 8])
        rot_d = self.din("c_rot_in", [128, 4 * T], BF16)

        maskt = self.alloc(f"l1f{s}_maskt", [128, 3 * 128], BF16)
        self.dma_in(maskt[:, :], self.din("c_maskt_in", [128, 3 * 128], BF16),
                    [], [("maskt",)])
        svb = self.alloc(f"l1f{s}_svb", [128, 8 * 4], F32)
        rw = self.alloc(f"l1f{s}_rw", [2, 2 * 512], F32)
        att = self.attsc

        for tb in range(NTB):
            self.ln_full(lambda c, tb=tb: self.xs_f(c, tb),
                         lambda c, tb=tb: ("xs", c, tb), lane=tb % 2)
            for c in range(NC):
                self.dma_out(self.spl[c, :, tb * 512:(tb + 1) * 512], self.xs_f(c, tb),
                             reads=[("xs", c, tb)], writes=[("spl", s, c, tb)])
            for c in range(NC):
                fs = self.fsc[c % 2]
                xap = self.xs_f(c, tb)
                p.step("dve", lambda e, xap=xap, fs=fs: e.tensor_mul(
                    fs[:, :], xap, ps[6][:, :]),
                    reads=[("xs", c, tb), ("ps", 6)], writes=[("fsc", c % 2)])
                p.step("dve", lambda e, xap=xap, fs=fs: e.tensor_sub(
                    xap, fs[:, :], ps[7][:, :]),
                    reads=[("fsc", c % 2), ("ps", 7), ("spl", s, c, tb)],
                    writes=[("xs", c, tb)])

        qsv = self.alloc(f"l1f{s}_qsv", [128, 4 * 6], F32)
        self.dma_in(qsv[:, :], qsks_d, [],
                    [("qsv", 0), ("qsv", 1)])
        for li in range(6):
            self._l1f_layer(s, li, rot_d, qsv, svb, rw, maskt,
                            wqkv_d, wo_d, g_d, b_d, w1a_d, w1g_d, w2_d,
                            b1a_d, b1g_d, b2_d)

        for tb in range(NTB):
            self.ln_full(lambda c, tb=tb: self.xs_f(c, tb),
                         lambda c, tb=tb: ("xs", c, tb), lane=tb % 2)
            for c in range(NC):
                f0, f2 = self.fsc[0], self.sq2[0]
                xap = self.xs_f(c, tb)
                p.step("dve", lambda e, xap=xap: e.tensor_mul(
                    f0[:, :], xap, ps[6][:, :]),
                    reads=[("xs", c, tb), ("ps", 6)], writes=[("fsc", 0)])
                p.step("dve", lambda e: e.tensor_sub(f0[:, :], f0[:, :], ps[7][:, :]),
                       reads=[("fsc", 0), ("ps", 7)], writes=[("fsc", 0)])
                self.dma_in(f2[:, :], self.spl[c, :, tb * 512:(tb + 1) * 512],
                            reads=[("spl", s, c, tb)], writes=[("sq2", 0)])
                p.step("gp", lambda e, xap=xap: e.tensor_add(
                    xap, f0[:, :], f2[:, :]),
                    reads=[("fsc", 0), ("sq2", 0)], writes=[("xs", c, tb)])
        self.free(f"l1f{s}_qsv", f"l1f{s}_rw", f"l1f{s}_svb", f"l1f{s}_maskt")
        self.p.fence_all()

    def _l1f_layer(self, s, li, rot_d, qsv, svb, rw, maskt, wqkv_d, wo_d,
                   g_d, b_d, w1a_d, w1g_d, w2_d, b1a_d, b1g_d, b2_d):
        p, ps, psbf = self.p, self.ps, self.psbf
        T, TG, NTB = self.T, self.TG, self.NTB
        att = self.attsc
        u = f"l1f{s}_{li}"
        rotc = self.alloc(u + "_rotc", [128, 4 * T], BF16)
        self.dma_in(rotc[:, :], rot_d, [], [("rotc",)])
        qs8 = self.alloc(u + "_qs8", [128, 4], F32)
        p32s = self.alloc(u + "_p32s", [128, 2 * 128], BF16)
        p.step("dve", lambda e, li=li: e.tensor_scalar_mul(
            qs8[:, 0:1], qsv[:, li * 4: li * 4 + 1], 8.0),
            reads=[("qsv", 0)], writes=[("qs8", 0)])
        p.step("dve", lambda e, li=li: e.tensor_copy(
            qs8[:, 1:2], qsv[:, li * 4 + 1: li * 4 + 2]),
            reads=[("qsv", 1)], writes=[("qs8", 1)])
        for qk in (0, 1):
            p.step("dve", lambda e, qk=qk: e.tensor_scalar_mul(
                p32s[:, qk * 128:(qk + 1) * 128], self.p32[:, :], qs8[:, qk:qk + 1]),
                reads=[("p32",), ("qs8", qk)], writes=[("p32s", qk)])

        wq = [self.alloc(u + f"_wq{i}", [128, 8 * 256], BF16) for i in range(2)]
        xhb1 = self.alloc(u + "_xhb1", [128, 8 * 512], BF16)
        qb = self.alloc(u + "_qb", [128, 8 * 1024], BF16)
        kb = self.alloc(u + "_kb", [128, 8 * 1280], BF16)
        vt = self.alloc(u + "_vt", [128, 10 * 1040], BF16)
        ytb = self.alloc(u + "_ytb", [128, 2 * 512], BF16)
        scr = self.alloc(u + "_scr", [128, 2048], F32, side="right")
        scrb = scr.bitcast(BF16)
        qs1 = scr[:, 0:1024]
        sqs = scrb[:, 2048:3072]
        ub = scrb[:, 3072:4096]
        wqkv_r = self.wscr["qkv"][li].rearrange("p (c o) -> p c o", o=3072)
        wo_r = self.wscr["wo"][li].rearrange("p (c o) -> p c o", o=1024)
        qkv_rd = self._wsc_keys(li, "qkv")
        wo_rd = self._wsc_keys(li, "wo")
        w2_rd = self._wsc_keys(li, "w2")
        QSK = [("qs1", 0), ("qs1", 1), ("sqs", 0), ("sqs", 1),
               ("ub", 0), ("ub", 1)]
        PBK = [("pb", pr) for pr in range(4)] + [("ptb", pr) for pr in range(4)]
        vt65 = vt.rearrange("p (t d) -> p t d", d=65)
        p.step("dve", lambda e: e.memset(vt65[:, :, 64:65], 1.0),
               [], [("vt1",)])
        for bnd in range(1, NTB):
            for c in range(4, NC):
                src = self.xs[:, c * TG + G + bnd * 512 - 1: c * TG + G + bnd * 512]
                p.step("dve", lambda e, src=src, c=c, bnd=bnd: e.tensor_copy(
                    svb[:, c * 4 + bnd: c * 4 + bnd + 1], src),
                    reads=[("xs", c, bnd - 1)], writes=[("svb", c, bnd)])
        for h in (0, 1):
            if h == 1:
                p.claim(PBK, QSK)
                p.claim([("yfa", c, t) for c in range(NC) for t in (0, 1)],
                        [("qb", c, t) for c in range(NC) for t in (0, 1)])
                kb3 = kb.rearrange("p (c x) -> p c x", x=1280)
                p.step("dve", lambda e, kb3=kb3: e.tensor_copy(
                    kb3[:, :, 0:256], kb3[:, :, 1024:1280]),
                    reads=[("kb", c, 1) for c in range(NC)], writes=[("kbt",)])
                p.step("dve", lambda e: e.tensor_copy(
                    vt[:, 0:2 * 1040], vt[:, 8 * 1040:10 * 1040]),
                    reads=[("vt", tc, c) for tc in (8, 9) for c in range(NC)]
                    + [("vt1",)],
                    writes=[("vt", tc, c) for tc in (0, 1) for c in range(NC)]
                    + [("vt1",)])
            for tbrel in (0, 1):
                tb = 2 * h + tbrel
                dstb = self.xhb if tbrel == 0 else xhb1
                dkey = ("xhb",) if tbrel == 0 else ("xh2",)
                for c in range(NC):
                    if c < 4:
                        src = self.xs_f(c, tb)
                    else:
                        src = self.xs[:, c * TG + G + tb * 512 - 1:
                                      c * TG + G + tb * 512 + 511]
                    rd = [("xs", c, tb)] \
                        + ([("xs", c, tb - 1)] if (c >= 4 and tb > 0) else []) \
                        + ([("xsg", c)] if (c >= 4 and tb == 0) else [])
                    p.step("dve", lambda e, src=src, c=c, dstb=dstb: e.tensor_copy(
                        dstb[:, c * 512:(c + 1) * 512], src),
                        reads=rd, writes=[dkey + (c,)])
                    if c >= 4 and tb > 0:
                        p.step("dve", lambda e, c=c, tb=tb, dstb=dstb: e.tensor_copy(
                            dstb[:, c * 512: c * 512 + 1],
                            svb[:, c * 4 + tb: c * 4 + tb + 1]),
                            reads=[("svb", c, tb)], writes=[dkey + (c,)])
            def mk_qkv_iter(og, ot, tbrel, j):
                o = og * 2 + ot
                tb = 2 * h + tbrel
                bank = j % 2
                wbuf = wq[og % 2]
                xsrc = self.xhb if tbrel == 0 else xhb1
                xkey = ("xhb",) if tbrel == 0 else ("xh2",)
                pr = j % 2
                c0 = pr * 512
                q1 = qs1[:, c0:c0 + 512]
                sq_ = sqs[:, c0:c0 + 512]
                ub_ = ub[:, c0:c0 + 512]
                rw_ = rw[:, c0:c0 + 512]
                bg = 2 if pr == 0 else 5
                bp = 3 if pr == 0 else 7

                def chain():
                    def mm(e):
                        r = None
                        for k in range(NC):
                            r = e.matmul(ps[bank][:, :],
                                         wbuf[:, k * 256 + ot * 128: k * 256 + ot * 128 + 128],
                                         xsrc[:, k * 512:(k + 1) * 512],
                                         start=(k == 0), stop=(k == NC - 1))
                        return r
                    p.step("pe", mm,
                           reads=[("wq", og % 2)] + [xkey + (c,) for c in range(NC)],
                           writes=[("ps", bank)])
                if o < 16:
                    qk = 0 if o < 8 else 1
                    m = o - 8 * qk
                    if qk == 0:
                        dst = qb[:, m * 1024 + tbrel * 512:
                                 m * 1024 + tbrel * 512 + 512]
                        dkey = ("qb", m, tbrel)
                    else:
                        dst = kb[:, m * 1280 + 256 + tbrel * 512:
                                 m * 1280 + 256 + tbrel * 512 + 512]
                        dkey = ("kb", m, tbrel)
                    Ai, Bi = (0, 1) if qk == 0 else (2, 3)

                    def S():
                        chain()
                        p.step("act", lambda e: e.activation(
                            q1, ps[bank][:, :], AF.Copy),
                            reads=[("ps", bank)], writes=[("qs1", pr)])
                        p.step("act", lambda e: e.activation(
                            sq_, ps[bank][:, :], AF.Square),
                            reads=[("ps", bank)], writes=[("sqs", pr)])

                    def Tst():
                        p.step("pe", lambda e: e.matmul(
                            ps[bg][0:2, :], self.grpsel[:, :], sq_,
                            start=True, stop=True),
                            reads=[("sqs", pr), ("grpsel",)], writes=[("ps", bg)])
                        p.step("act", lambda e: e.activation(
                            rw_, ps[bg][0:2, :], AF.Ln, bias=self.eps2[:, :]),
                            reads=[("ps", bg), ("eps2",)], writes=[("rw", pr)])
                        p.step("act", lambda e: e.activation(
                            rw_, rw_, AF.Exp, scale=-0.5),
                            reads=[("rw", pr)], writes=[("rw", pr)])
                        p.step("dve", lambda e: e.tensor_scalar_min(
                            rw_, rw_, 1e12),
                            reads=[("rw", pr)], writes=[("rw", pr)])
                        p.step("gp", lambda e: e.tensor_tensor(
                            ub_, q1,
                            rotc[:, Bi * T + tb * 512: Bi * T + tb * 512 + 512],
                            ALU.mult),
                            reads=[("qs1", pr), ("rotc",)], writes=[("ub", pr)])

                    def U():
                        p.step("pe", lambda e: e.matmul(
                            ps[bg][:, :], self.sel2[:, :], rw_,
                            start=True, stop=True),
                            reads=[("rw", pr), ("sel2",)], writes=[("ps", bg)])
                        p.step("pe", lambda e: e.matmul(
                            ps[bp][:, :], p32s[:, qk * 128:(qk + 1) * 128], ub_,
                            start=True, stop=True),
                            reads=[("ub", pr), ("p32s", qk)], writes=[("ps", bp)])
                        p.step("dve", lambda e: e.scalar_tensor_tensor(
                            q1, q1, qs8[:, qk:qk + 1],
                            rotc[:, Ai * T + tb * 512: Ai * T + tb * 512 + 512],
                            ALU.mult, ALU.mult),
                            reads=[("qs1", pr), ("rotc",), ("qs8", qk)],
                            writes=[("qs1", pr)])
                        p.step("dve", lambda e: e.tensor_add(
                            q1, q1, ps[bp][:, :]),
                            reads=[("qs1", pr), ("ps", bp)], writes=[("qs1", pr)])
                        p.step("dve", lambda e: e.tensor_mul(
                            dst, q1, ps[bg][:, :]),
                            reads=[("qs1", pr), ("ps", bg)], writes=[dkey])
                else:
                    chv = o - 16

                    def S():
                        chain()
                        p.step("act", lambda e: e.activation(
                            ub_, ps[bank][:, :], AF.Copy),
                            reads=[("ps", bank)], writes=[("ub", pr)])

                    def Tst():
                        def tr(e):
                            r = None
                            for i in range(4):
                                r = e.transpose(psbf[4][:, pr * 512 + i * 128:
                                                        pr * 512 + (i + 1) * 128],
                                                ub_[:, i * 128:(i + 1) * 128],
                                                self.ident[:, :])
                            return r
                        p.step("pe", tr, reads=[("ub", pr), ("ident",)],
                               writes=[("ps", 4)])

                    def U():
                        src4 = psbf[4][:, pr * 512: pr * 512 + 512].rearrange(
                            "p (b g x) -> p b g x", g=2, x=64)
                        vt5 = vt.rearrange("p (t c g d) -> p t c g d",
                                           c=8, g=2, d=65)
                        dst4 = vt5[:, 2 + tbrel * 4: 6 + tbrel * 4, chv, :, 0:64]
                        p.step("dve", lambda e: e.tensor_copy(dst4, src4),
                               reads=[("ps", 4)],
                               writes=[("vt", 2 + tbrel * 4 + i, chv)
                                       for i in range(4)])
                return S, Tst, U

            qstages = []
            dmas = {}
            j = 0
            for og in range(12):
                dmas[j] = og
                for ot in range(2):
                    for tbrel in range(2):
                        qstages.append(mk_qkv_iter(og, ot, tbrel, j))
                        j += 1
            for j in range(len(qstages) + 2):
                if j >= 2:
                    qstages[j - 2][2]()
                if 1 <= j <= len(qstages):
                    qstages[j - 1][1]()
                if j < len(qstages):
                    if j in dmas:
                        og = dmas[j]
                        self.dma_in(wq[og % 2][:, :],
                                    wqkv_r[:, :, og * 256:(og + 1) * 256],
                                    reads=qkv_rd, writes=[("wq", og % 2)])
                    qstages[j][0]()
            p.claim(QSK, PBK)

            def mk_attn_iter(ch, tbrel, gb, qt4, po2, pr):
                po = po2 * 64
                qa0 = tbrel * 512 + qt4 * 128
                kc0 = 0
                if h == 0 and tbrel == 0:
                    kc0 = 2 if qt4 == 0 else (1 if qt4 == 1 else 0)
                a0 = pr * 4
                pb_ = scrb[:, pr * 384:(pr + 1) * 384]
                ptb_ = scrb[:, 1536 + pr * 384: 1536 + (pr + 1) * 384]
                bt = 4 + pr % 2
                tofs = (pr // 2) * 384
                vofs = pr * 128
                r0 = qa0 + kc0 * 128
                krd = []
                if r0 < 256:
                    krd.append(("kbt",))
                if r0 < 768 and qa0 + 384 > 256:
                    krd.append(("kb", ch, 0))
                if qa0 + 384 > 768:
                    krd.append(("kb", ch, 1))

                def S():
                    def mm(e):
                        return e.matmul(
                            ps[pr][:, kc0 * 128:384],
                            qb[po:po + 64,
                               ch * 1024 + qa0: ch * 1024 + qa0 + 128],
                            kb[po:po + 64,
                               ch * 1280 + r0: ch * 1280 + qa0 + 384],
                            start=True, stop=True)
                    p.step("pe", mm, reads=[("qb", ch, tbrel)] + krd,
                           writes=[("ps", pr)])
                    p.step("dve", lambda e: e.reduce_max(
                        att[:, a0:a0 + 1], ps[pr][:, kc0 * 128:384],
                        axis=AX.X, negate=True),
                        reads=[("ps", pr)], writes=[("att", a0)])
                    p.step("act", lambda e: e.activation(
                        pb_[:, kc0 * 128:384], ps[pr][:, kc0 * 128:384],
                        AF.Exp, bias=att[:, a0:a0 + 1]),
                        reads=[("ps", pr), ("att", a0)], writes=[("pb", pr)])

                def Tst():
                    def tr(e):
                        r = None
                        for kc in range(kc0, 3):
                            r = e.transpose(
                                psbf[bt][:, tofs + kc * 128: tofs + kc * 128 + 128],
                                pb_[:, kc * 128:(kc + 1) * 128],
                                self.ident[:, :])
                        return r
                    p.step("pe", tr, reads=[("pb", pr), ("ident",)],
                           writes=[("ps", bt)])
                    p.step("dve", lambda e: e.tensor_tensor(
                        ptb_[:, kc0 * 128:384],
                        psbf[bt][:, tofs + kc0 * 128: tofs + 384],
                        maskt[:, kc0 * 128:384], ALU.mult),
                        reads=[("ps", bt), ("maskt",)], writes=[("ptb", pr)])

                def V():
                    def pv(e):
                        r = None
                        for kc in range(kc0, 3):
                            tc = qt4 + tbrel * 4 + kc
                            r = e.matmul(
                                ps[6][:, vofs: vofs + 65],
                                ptb_[:, kc * 128:(kc + 1) * 128],
                                vt[:, tc * 1040 + ch * 130 + po2 * 65:
                                   tc * 1040 + ch * 130 + po2 * 65 + 65],
                                start=(kc == kc0), stop=(kc == 2),
                                skip_group_check=True)
                        return r
                    p.step("pe", pv,
                           reads=[("ptb", pr), ("vt1",)]
                           + [("vt", qt4 + tbrel * 4 + kc, ch)
                              for kc in range(kc0, 3)],
                           writes=[("ps", 6)])
                    p.step("dve", lambda e: e.reciprocal(
                        att[:, a0 + 1:a0 + 2],
                        ps[6][:, vofs + 64: vofs + 65]),
                        reads=[("ps", 6)], writes=[("att", a0 + 1)])
                    dst = ytb[:, gb * 512 + qt4 * 128 + po:
                              gb * 512 + qt4 * 128 + po + 64]
                    p.step("act", lambda e: e.activation(
                        dst, ps[6][:, vofs: vofs + 64], AF.Copy,
                        scale=att[:, a0 + 1:a0 + 2]),
                        reads=[("ps", 6), ("att", a0 + 1)],
                        writes=[("ytb", gb, qt4, po2)])
                return S, Tst, V

            def mk_group_post(ch, tbrel, gb):
                def post():
                    def tr(e):
                        r = None
                        for i in range(4):
                            r = e.transpose(psbf[0][:, i * 128:(i + 1) * 128],
                                            ytb[:, gb * 512 + i * 128:
                                                gb * 512 + i * 128 + 128],
                                            self.ident[:, :])
                        return r
                    p.step("pe", tr,
                           reads=[("ytb", gb, i, po2)
                                  for i in range(4) for po2 in (0, 1)]
                           + [("ident",)],
                           writes=[("ps", 0)])
                    p.claim([("qb", ch, tbrel)], [("yfa", ch, tbrel)])
                    dsty = qb[:, ch * 1024 + tbrel * 512:
                              ch * 1024 + tbrel * 512 + 512]
                    p.step("dve", lambda e: e.tensor_copy(
                        dsty, psbf[0][:, 0:512]),
                        reads=[("ps", 0)], writes=[("yfa", ch, tbrel)])
                return post

            stages = []
            it = 0
            for ch in range(NC):
                for tbrel in range(2):
                    gb = (ch * 2 + tbrel) % 2
                    for qt4 in range(4):
                        for po2 in range(2):
                            S, Tst, V = mk_attn_iter(ch, tbrel, gb, qt4, po2,
                                                     it % 4)
                            it += 1
                            post = mk_group_post(ch, tbrel, gb) \
                                if (qt4 == 3 and po2 == 1) else None
                            stages.append((S, Tst, V, post))
            for i in range(len(stages) + 2):
                if i < len(stages):
                    stages[i][0]()
                if 1 <= i <= len(stages):
                    stages[i - 1][1]()
                if i >= 2:
                    stages[i - 2][2]()
                    if stages[i - 2][3] is not None:
                        stages[i - 2][3]()
            for og2 in range(4):
                wbuf = wq[og2 % 2]
                self.dma_in(wbuf[:, :], wo_r[:, :, og2 * 256:(og2 + 1) * 256],
                            reads=wo_rd, writes=[("wq", og2 % 2)])
                for ot in range(2):
                    o = og2 * 2 + ot
                    for tbrel in range(2):
                        tb = 2 * h + tbrel
                        bank = ot * 2 + tbrel
                        def mm(e, ot=ot, bank=bank, wbuf=wbuf, tbrel=tbrel):
                            r = None
                            for k in range(NC):
                                r = e.matmul(ps[bank][:, :],
                                             wbuf[:, k * 256 + ot * 128: k * 256 + ot * 128 + 128],
                                             qb[:, k * 1024 + tbrel * 512:
                                                k * 1024 + tbrel * 512 + 512],
                                             start=(k == 0), stop=(k == NC - 1))
                            return r
                        p.step("pe", mm,
                               reads=[("wq", og2 % 2)]
                               + [("yfa", k, tbrel) for k in range(NC)],
                               writes=[("ps", bank)])
                        xap = self.xs_f(o, tb)
                        p.step("dve", lambda e, xap=xap, bank=bank: e.tensor_add(
                            xap, xap, ps[bank][:, :]),
                            reads=[("ps", bank), ("xs", o, tb)],
                            writes=[("xs", o, tb)])
        self.free(u + "_scr", u + "_ytb", u + "_vt", u + "_kb", u + "_qb",
                  u + "_xhb1", u + "_wq1", u + "_wq0",
                  u + "_p32s", u + "_qs8", u + "_rotc")

        xn = self.alloc(u + "_xn", [128, 8 * 1024], BF16)
        hB = self.alloc(u + "_h", [128, NKI * 1024], BF16)
        w1x = [self.alloc(u + f"_w1x{i}", [128, 2 * 1024], BF16) for i in range(2)]
        w2b2 = [self.alloc(u + f"_w2b{i}", [128, NKI * 256], BF16)
                for i in range(2)]
        gsb = self.alloc(u + "_gsb", [128, 512], F32)
        bia = self.alloc(u + "_bia", [128, NKI], F32)
        big = self.alloc(u + "_big", [128, NKI], F32)
        bi2 = self.alloc(u + "_bi2", [128, 8], F32)
        lng = self.alloc(u + "_lng", [128, 8], F32)
        lnb = self.alloc(u + "_lnb", [128, 8], F32)
        self.dma_in(bia[:, :], b1a_d[li], [], [("bia",)])
        self.dma_in(big[:, :], b1g_d[li], [], [("big",)])
        self.dma_in(bi2[:, :], b2_d[li], [], [("bi2",)])
        self.dma_in(lng[:, :], g_d[li], [], [("lng",)])
        self.dma_in(lnb[:, :], b_d[li], [], [("lnb",)])
        for bnd in range(1, NTB):
            for c in range(4, NC):
                src = self.xs[:, c * TG + G + bnd * 512 - 1: c * TG + G + bnd * 512]
                p.step("dve", lambda e, src=src, c=c, bnd=bnd: e.tensor_copy(
                    svb[:, c * 4 + bnd: c * 4 + bnd + 1], src),
                    reads=[("xs", c, bnd - 1)], writes=[("svb", c, bnd)])
        for th in range(max(1, NTB // 2)):
            tbs = [tb for tb in (2 * th, 2 * th + 1) if tb < NTB]
            for tb in tbs:
                for c in range(NC):
                    if c < 4:
                        sap = self.xs_f(c, tb)
                    else:
                        sap = self.xs[:, c * TG + G + tb * 512 - 1:
                                      c * TG + G + tb * 512 + 511]
                    rd = [("xs", c, tb)] + ([("xs", c, tb - 1)] if (c >= 4 and tb > 0) else []) + ([("xsg", c)] if (c >= 4 and tb == 0) else [])
                    p.step("dve", lambda e, c=c, sap=sap: e.tensor_copy(
                        self.xhb[:, c * 512:(c + 1) * 512], sap),
                        reads=rd, writes=[("xhb", c)])
                    if c >= 4 and tb > 0:
                        p.step("dve", lambda e, c=c, tb=tb: e.tensor_copy(
                            self.xhb[:, c * 512: c * 512 + 1],
                            svb[:, c * 4 + tb: c * 4 + tb + 1]),
                            reads=[("svb", c, tb)], writes=[("xhb", c)])
                self.ln_full(lambda c: self.xhb[:, c * 512:(c + 1) * 512],
                             lambda c: ("xhb", c), lane=tb % 2, src_bf=True)
                for c in range(NC):
                    fs = self.fsc[c % 2]
                    xh = self.xhb[:, c * 512:(c + 1) * 512]
                    p.step("dve", lambda e, xh=xh, fs=fs: e.tensor_mul(
                        fs[:, :], xh, ps[6][:, :]),
                        reads=[("xhb", c), ("ps", 6)], writes=[("fsc", c % 2)])
                    p.step("dve", lambda e, fs=fs: e.tensor_sub(
                        fs[:, :], fs[:, :], ps[7][:, :]),
                        reads=[("fsc", c % 2), ("ps", 7)], writes=[("fsc", c % 2)])
                    dst = xn[:, c * 1024 + (tb % 2) * 512: c * 1024 + (tb % 2) * 512 + 512]
                    p.step("dve", lambda e, fs=fs, dst=dst, c=c: e.tensor_scalar(
                        dst, fs[:, :], lng[:, c:c + 1], lnb[:, c:c + 1],
                        ALU.mult, ALU.add),
                        reads=[("fsc", c % 2), ("lng",), ("lnb",)],
                        writes=[("xn", c, tb)])
            for hc in range(NKI):
                wbuf = w1x[hc % 2]
                self.dma_in(wbuf[:, 0:1024],
                            self.wscr["w1a"][li][:, hc * 1024:(hc + 1) * 1024],
                            reads=[("wsc", li, "w1a", hc * 1024 // self.pcw)],
                            writes=[("w1x", hc % 2)])
                self.dma_in(wbuf[:, 1024:2048],
                            self.wscr["w1g"][li][:, hc * 1024:(hc + 1) * 1024],
                            reads=[("wsc", li, "w1g", hc * 1024 // self.pcw)],
                            writes=[("w1x", hc % 2, 1)])
                for ti, tb in enumerate(tbs):
                    j4 = (hc * 2 + ti) % 4
                    ba, bg = 2 * j4, 2 * j4 + 1
                    def mma(e, wbuf=wbuf, ba=ba, bg=bg, tb=tb):
                        r = None
                        for k in range(NC):
                            e.matmul(ps[ba][:, :],
                                     wbuf[:, k * 128: k * 128 + 128],
                                     xn[:, k * 1024 + (tb % 2) * 512:
                                        k * 1024 + (tb % 2) * 512 + 512],
                                     start=(k == 0), stop=(k == NC - 1),
                                     skip_group_check=True)
                            r = e.matmul(ps[bg][:, :],
                                         wbuf[:, 1024 + k * 128: 1024 + k * 128 + 128],
                                         xn[:, k * 1024 + (tb % 2) * 512:
                                            k * 1024 + (tb % 2) * 512 + 512],
                                         start=(k == 0), stop=(k == NC - 1),
                                         skip_group_check=True)
                        return r
                    p.step("pe", mma,
                           reads=[("w1x", hc % 2), ("w1x", hc % 2, 1)]
                           + [("xn", c, tb) for c in range(NC)],
                           writes=[("ps", ba), ("ps", bg)])
                    p.step("act", lambda e, bg=bg, hc=hc: e.activation(
                        gsb[:, :], ps[bg][:, :], AF.Gelu, bias=big[:, hc:hc + 1]),
                        reads=[("ps", bg), ("big",)], writes=[("gsb",)])
                    dst = hB[:, hc * 1024 + (tb % 2) * 512:
                             hc * 1024 + (tb % 2) * 512 + 512]
                    p.step("dve", lambda e, dst=dst, ba=ba, hc=hc: e.scalar_tensor_tensor(
                        dst, ps[ba][:, :], bia[:, hc:hc + 1], gsb[:, :],
                        ALU.add, ALU.mult),
                        reads=[("ps", ba), ("bia",), ("gsb",)],
                        writes=[("h", hc, tb)])
            for og in range(4):
                w2i = (th * 4 + og) % 2
                w2b = w2b2[w2i]
                self.dma_in(w2b[:, :],
                            self.wscr["w2"][li].rearrange("p (k o) -> p k o", o=1024)
                            [:, :, og * 256:(og + 1) * 256],
                            reads=w2_rd, writes=[("w2b", w2i)])
                for ti, tb in enumerate(tbs):
                    for ot in range(2):
                        o = og * 2 + ot
                        bank = 4 + ot * 2 + ti
                        def mm(e, ot=ot, bank=bank, tb=tb, w2b=w2b):
                            r = None
                            for k in range(NKI):
                                r = e.matmul(
                                    ps[bank][:, :],
                                    w2b[:, k * 256 + ot * 128: k * 256 + ot * 128 + 128],
                                    hB[:, k * 1024 + (tb % 2) * 512:
                                       k * 1024 + (tb % 2) * 512 + 512],
                                    start=(k == 0), stop=(k == NKI - 1))
                            return r
                        p.step("pe", mm,
                               reads=[("w2b", w2i)] + [("h", k, tb) for k in range(NKI)],
                               writes=[("ps", bank)])
                        xap = self.xs_f(o, tb)
                        p.step("dve", lambda e, xap=xap, bank=bank, o=o:
                               e.scalar_tensor_tensor(
                                   xap, ps[bank][:, :], bi2[:, o:o + 1], xap,
                                   ALU.add, ALU.add),
                               reads=[("ps", bank), ("bi2",), ("xs", o, tb)],
                               writes=[("xs", o, tb)])
        self.free(u + "_lnb", u + "_lng", u + "_bi2", u + "_big", u + "_bia",
                  u + "_gsb", u + "_w2b1", u + "_w2b0", u + "_w1x1", u + "_w1x0",
                  u + "_h", u + "_xn")





def _feed_for(mk, w, th, x_fm):
    feed = {"x_in": x_fm}
    cmap = {"c_cmask_in": "cmask", "c_bmask_in": "bmask", "c_ident_in": "ident",
            "c_p32_in": "p32", "c_grpsel_in": "grpsel", "c_sel2_in": "sel2",
            "c_rot_in": "rot", "c_maskt_in": "maskt"}
    for n in mk.in_names:
        if n == "x_in":
            continue
        feed[n] = th[cmap[n]] if n in cmap else w[n]
    return feed


def kernel(**inputs):
    T = 2048
    x = np.ascontiguousarray(np.asarray(inputs["L0_attn_x"], np.float32))
    B = x.shape[0]
    ncores = 8
    spb = B // ncores
    w = prep_weights(inputs)
    th = host_tables(T)
    mk = MK(T=T, blocks=("l0a", "l0f", "l1a", "l1f"), nseq=spb)
    mk.p.emit()
    xr = x.reshape(ncores, spb, T, NC, 128).transpose(0, 1, 3, 4, 2)
    in_maps = [_feed_for(mk, w, th, np.ascontiguousarray(xr[c])) for c in range(ncores)]
    res = run_bass_kernel_spmd(mk.nc, in_maps, core_ids=list(range(ncores)))
    out = np.empty((ncores, spb, T, D), np.float32)
    for c in range(ncores):
        y = res.results[c]["y_out"]
        out[c] = y.transpose(0, 3, 1, 2).reshape(spb, T, D)
    return out.reshape(B, T, D)
